# version 34
# baseline (speedup 1.0000x reference)
import numpy as np
import ml_dtypes
from contextlib import ExitStack

import concourse.bass as bass
import concourse.mybir as mybir
from concourse.bass_utils import run_bass_kernel_spmd

F32 = mybir.dt.float32
BF16 = mybir.dt.bfloat16
AF = mybir.ActivationFunctionType
ALU = mybir.AluOpType
AXX = mybir.AxisListType.X

NCORES = 8
T = 1024
NT = 8
D = 2048
NK = 16
H = 16
DH = 128
M = 64
EPS1 = 1e-5
EPS2 = 1e-6
BF = ml_dtypes.bfloat16

ENG_SEM = {"tensor": "pe", "vector": "dve", "scalar": "act", "gpsimd": "pool", "sync": "spc"}
PER_DMA_SEMS = False
STREAM_EXCLUDE = set()


class Op:
    __slots__ = ("eng", "fn", "dma", "deps", "sem", "val", "marked", "stream", "idx")

    def __init__(self, eng, fn, dma, deps, stream=None, idx=0):
        self.eng, self.fn, self.dma, self.deps = eng, fn, dma, deps
        self.sem = None
        self.val = None
        self.marked = False
        self.stream = stream
        self.idx = idx


class Prog:

    def __init__(self, nc):
        self.nc = nc
        self.ops = []
        self.res = {}
        self.streams = {}
        self.ndma = 0

    def _edges(self, reads, writes, waits):
        deps = set()
        for o in waits:
            if o is not None:
                deps.add(o)
        for rn in reads:
            r = self.res.setdefault(rn, {"w": None, "r": []})
            if r["w"] is not None:
                deps.add(r["w"])
        for wn in writes:
            r = self.res.setdefault(wn, {"w": None, "r": []})
            if r["w"] is not None:
                deps.add(r["w"])
            for o in r["r"]:
                deps.add(o)
        return deps

    def _add(self, eng, fn, dma, reads, writes, waits, stream=None):
        op = Op(eng, fn, dma, self._edges(reads, writes, waits), stream, len(self.ops))
        self.ops.append(op)
        for rn in reads:
            self.res[rn]["r"].append(op)
        for wn in writes:
            self.res[wn]["w"] = op
            self.res[wn]["r"] = []
        return op

    def comp(self, eng, fn, reads=(), writes=(), waits=()):
        return self._add(eng, fn, False, reads, writes, waits)

    def dma(self, queue, fn, reads=(), writes=(), waits=(), sem=None, barrier=False):
        if sem is None:
            sem = f"u{self.ndma}"
            self.ndma += 1
        st = self.streams.setdefault(sem, {"count": 0, "barrier": barrier})
        st["count"] += 16
        op = self._add(queue, fn, True, reads, writes, waits, stream=sem)
        op.val = st["count"]
        return op

    def emit(self, ctx):
        nc = self.nc
        def skip_edge(dep, op):
            return (not dep.dma) and (not op.dma) and dep.eng == op.eng == "tensor"

        for op in self.ops:
            best = {}
            for dep in op.deps:
                if skip_edge(dep, op):
                    continue
                key = ("s", dep.stream) if dep.dma else ("e", dep.eng)
                cur = best.get(key)
                if cur is None or dep.idx > cur.idx:
                    best[key] = dep
            op.deps = set(best.values())
            for dep in op.deps:
                dep.marked = True
        counters = {}
        handles = {}

        def get_sem(name):
            if name not in handles:
                handles[name] = ctx.enter_context(nc.semaphore(name))
                counters[name] = 0
            return name

        for op in self.ops:
            if op.dma:
                op.sem = get_sem(op.stream)
            elif op.marked:
                name = get_sem(ENG_SEM[op.eng])
                counters[name] += 1
                op.sem = name
                op.val = counters[name]

        with nc.Block() as block:
            for engname in ("tensor", "vector", "scalar", "gpsimd", "sync"):
                deco = getattr(block, engname)

                def make(engname=engname):
                    def body(e):
                        waited = {}
                        for op in self.ops:
                            if op.eng != engname:
                                continue
                            wants = {}
                            for dep in op.deps:
                                if skip_edge(dep, op):
                                    continue
                                assert dep.sem is not None
                                v = dep.val
                                if (dep.dma and dep.stream is not None
                                        and self.streams[dep.stream]["barrier"]):
                                    v = self.streams[dep.stream]["count"]
                                if wants.get(dep.sem, 0) < v:
                                    wants[dep.sem] = v
                            for s in sorted(wants):
                                v = wants[s]
                                if waited.get(s, 0) >= v:
                                    continue
                                e.wait_ge(handles[s], v)
                                waited[s] = v
                            inst = op.fn(e)
                            if op.sem is not None:
                                inst.then_inc(handles[op.sem], 16 if op.dma else 1)
                    return body

                deco(make())


def r3(ap, inner):
    return ap.rearrange("p (a b) -> p a b", b=inner)


DEBUG = False


def build_launch1():
    nc = bass.Bass("TRN2", target_bir_lowering=False, detect_race_conditions=False)
    ctx = ExitStack()

    x_d = nc.dram_tensor("x", [T, D], F32, kind="ExternalInput").ap()
    wq_d = nc.dram_tensor("wq", [D, D], F32, kind="ExternalInput").ap()
    wk_d = nc.dram_tensor("wk", [D, D], F32, kind="ExternalInput").ap()
    wvT_d = nc.dram_tensor("wvT", [D, D], F32, kind="ExternalInput").ap()
    wfa_d = nc.dram_tensor("wfa", [2 * 128, D], F32, kind="ExternalInput").ap()
    woa_d = nc.dram_tensor("woa", [2 * 128, D], F32, kind="ExternalInput").ap()
    wmapqT_d = nc.dram_tensor("wmapqT", [DH, M], F32, kind="ExternalInput").ap()
    wmapkT_d = nc.dram_tensor("wmapkT", [DH, M], F32, kind="ExternalInput").ap()
    bq_c_d = nc.dram_tensor("bq_c", [DH, H], F32, kind="ExternalInput").ap()
    bk_c_d = nc.dram_tensor("bk_c", [DH, H], F32, kind="ExternalInput").ap()
    bv_r_d = nc.dram_tensor("bv_r", [1, D], F32, kind="ExternalInput").ap()
    lnw_c_d = nc.dram_tensor("lnw_c", [DH, NK], F32, kind="ExternalInput").ap()
    lnb_c_d = nc.dram_tensor("lnb_c", [DH, NK], F32, kind="ExternalInput").ap()
    ident_d = nc.dram_tensor("ident", [128, 128], BF16, kind="ExternalInput").ap()

    qp_d = nc.dram_tensor("qp", [128, 8 * T], BF16, kind="ExternalOutput").ap()
    xn2_d = nc.dram_tensor("xn2", [128, NK * T], BF16, kind="ExternalOutput").ap()
    kvm_d = nc.dram_tensor("kvm", [128, 1024], BF16, kind="ExternalOutput").ap()
    kvs_d = nc.dram_tensor("kvs", [128, 16], F32, kind="ExternalOutput").ap()
    amax_d = nc.dram_tensor("amax", [1, 1], F32, kind="ExternalOutput").ap()
    wfb_d = nc.dram_tensor("wfb", [128, 2 * D], BF16, kind="ExternalOutput").ap()
    wob_d = nc.dram_tensor("wob", [128, 2 * D], BF16, kind="ExternalOutput").ap()

    weq_d = nc.dram_tensor("weq", [128, NK * H * M], BF16, kind="Internal")
    wek_d = nc.dram_tensor("wek", [128, NK * H * M], BF16, kind="Internal")
    if DEBUG:
        kptm_dbg = nc.dram_tensor("kptm_dbg", [128, NT * 1024], BF16, kind="ExternalOutput").ap()
        stats_dbg = nc.dram_tensor("stats_dbg", [128, NT], F32, kind="ExternalOutput").ap()

    A = nc.alloc_sbuf_tensor
    ident = A("identt", [128, 128], BF16).ap()
    ones_col_b = A("ones_col_b", [128, 1], BF16).ap()
    ones_col_f = A("ones_col_f", [128, 1], F32).ap()
    ones_row_f = A("ones_row_f", [1, 128], F32).ap()
    ones_row_bb = A("ones_row_bb", [1, 128], BF16).ap()
    ones512_b = A("ones512_b", [1, 512], BF16).ap()
    eps1_col = A("eps1_col", [128, 1], F32).ap()
    nhalf_col = A("nhalf_col", [128, 1], F32).ap()
    wmapq = A("wmapq", [DH, M], BF16).ap()
    wmapk = A("wmapk", [DH, M], BF16).ap()
    bq_c = A("bq_cs", [DH, H], BF16).ap()
    bk_c = A("bk_cs", [DH, H], BF16).ap()
    bv_r = A("bv_rs", [1, D], BF16).ap()
    lnw_c = A("lnw_cs", [DH, NK], F32).ap()
    lnb_c = A("lnb_cs", [DH, NK], F32).ap()

    xs = [A(f"xs{j}", [128, D], BF16).ap() for j in range(3)]
    wfa = A("wfas", [128, D], BF16).ap()
    am_tmp = A("am_tmp", [128, 1], F32).ap()
    woa = A("woas", [128, D], BF16).ap()
    am_col = A("am_col", [128, 1], F32).ap()
    am_cb = A("am_cb", [128, 1], BF16).ap()
    am1 = A("am1", [1, 1], F32).ap()
    xn_tm = A("xn_tm", [128, NT * D], BF16).ap()
    whs = [A(f"whs{j}", [128, D], BF16).ap() for j in range(5)]
    stg = [A(f"stg{j}", [128, NK * 512], BF16).ap() for j in range(2)]
    effs = [A(f"effs{j}", [128, NK * 128], BF16).ap() for j in range(3)]

    st24 = [A(f"st24_{j}", [128, 24], F32).ap() for j in range(2)]
    mv = [A(f"mv{j}", [128, 2], F32).ap() for j in range(2)]
    r1c = [A(f"r1c{j}", [128, 1], F32).ap() for j in range(2)]
    tca = [A(f"tca{j}", [128, 1], F32).ap() for j in range(2)]
    tcb = [A(f"tcb{j}", [128, 1], F32).ap() for j in range(2)]
    r2cs = A("r2cs", [128, NT], F32).ap()

    xn_fm = A("xn_fm", [128, NK * T], BF16).ap()
    elu1 = A("elu1", [128, T], BF16).ap()
    qp = A("qps", [128, 2 * T], BF16).ap()
    kp_tm = A("kp_tm", [128, NT * 1024], BF16).ap()
    Ats = [A(f"Ats{j}", [128, 1024], BF16).ap() for j in range(2)]
    ksum_cf = A("ksum_cf", [M, H], F32).ap()
    ksum_rb = A("ksum_rb", [1, H * M], BF16).ap()
    kv_sb = A("kv_sbs", [128, 1040], F32).ap()
    diag = A("diagt", [128, 128], F32).ap()
    bq_row = A("bq_row", [1, H * M], BF16).ap()
    bk_row = A("bk_row", [1, H * M], BF16).ap()
    padq = [A(f"padq{j}", [128, 128], BF16).ap() for j in range(1)]

    B_r2 = A("B_r2", [128, T], F32).ap()
    r2row = A("r2row", [1, T], F32).ap()
    prep_st = [A(f"prep_st{j}", [128, NK * M], BF16).ap() for j in range(2)]

    APs = nc.alloc_psum_tensor
    ps_ab = APs("ps_ab", [128, 1024], F32).ap()
    ps_cd = APs("ps_cd", [128, 1024], F32).ap()
    ps_ef = APs("ps_ef", [128, 1024], F32).ap()
    ps_t1 = APs("ps_t1", [128, 1024], BF16).ap()
    ps_t2 = APs("ps_t2", [128, 1024], BF16).ap()

    P = Prog(nc)

    def cdma(dst, src):
        return lambda e: e.dma_start(out=dst, in_=src)

    P.dma("gpsimd", cdma(ident[:, :], ident_d), writes=["ident"], sem="csts", barrier=True)
    P.dma("gpsimd", cdma(wmapq[:, :], wmapqT_d), writes=["wmapq"], sem="csts", barrier=True)
    P.dma("gpsimd", cdma(wmapk[:, :], wmapkT_d), writes=["wmapk"], sem="csts", barrier=True)
    P.dma("gpsimd", cdma(bq_c[:, :], bq_c_d), writes=["bq_c"], sem="csts", barrier=True)
    P.dma("gpsimd", cdma(bk_c[:, :], bk_c_d), writes=["bk_c"], sem="csts", barrier=True)
    P.dma("gpsimd", cdma(bv_r[:, :], bv_r_d), writes=["bv_r"], sem="csts", barrier=True)
    P.dma("sync", cdma(lnw_c[:, :], lnw_c_d), writes=["lnw_c"], sem="cstsy", barrier=True)
    P.dma("sync", cdma(lnb_c[:, :], lnb_c_d), writes=["lnb_c"], sem="cstsy", barrier=True)

    P.comp("vector", lambda e: e.memset(ones_col_b[:, :], 1.0), writes=["ones_col_b"])
    P.comp("vector", lambda e: e.memset(ones_col_f[:, :], 1.0), writes=["ones_col_f"])
    P.comp("vector", lambda e: e.memset(ones_row_f[:, :], 1.0), writes=["ones_row_f"])
    P.comp("vector", lambda e: e.memset(ones_row_bb[:, :], 1.0), writes=["ones_row_bb"])
    P.comp("vector", lambda e: e.memset(ones512_b[:, :], 1.0), writes=["ones512_b"])
    P.comp("vector", lambda e: e.memset(eps1_col[:, :], EPS1), writes=["eps1_col"])
    P.comp("vector", lambda e: e.memset(nhalf_col[:, :], -0.5), writes=["nhalf_col"])

    def prep_blk(w_d, mapw, mres, eff_d, tag, blk, cd_only=False, waits=()):
        sslot = blk % 2
        wres = f"stg{sslot}"
        ldop = P.dma("gpsimd",
              (lambda sslot, blk, w_d: lambda e: e.dma_start(
                  out=r3(stg[sslot], D)[:, 0:2, :],
                  in_=w_d.rearrange("(b p) k -> p b k", p=128)[:, 2 * blk:2 * blk + 2, :]))(
                  sslot, blk, w_d),
              writes=[wres], sem=f"stg{sslot}", waits=waits)
        for hh in range(2):
            h = 2 * blk + hh
            psx, pre = (ps_cd, "ps_cd") if cd_only else (
                (ps_ab, "ps_ab") if hh % 2 == 0 else (ps_cd, "ps_cd"))
            for kc in range(NK):
                P.comp("tensor",
                       (lambda sslot, hh, kc, psx: lambda e: e.matmul(
                           psx[:, kc * M:(kc + 1) * M],
                           r3(stg[sslot], D)[:, hh, kc * 128:(kc + 1) * 128],
                           mapw[:, :], start=True, stop=True))(sslot, hh, kc, psx),
                       reads=[wres, mres], writes=[f"{pre}{kc // 8}"])
            pslot = hh % 2
            P.comp("scalar",
                   (lambda psx, pslot: lambda e: e.activation(
                       prep_st[pslot][:, :], psx[:, :], AF.Copy))(psx, pslot),
                   reads=[f"{pre}0", f"{pre}1"], writes=[f"prep_st{pslot}"])
            P.dma("sync",
                  (lambda pslot, h, eff_d: lambda e: e.dma_start(
                      out=r3(eff_d.ap(), H * M)[:, :, h * M:(h + 1) * M],
                      in_=r3(prep_st[pslot], M)[:, :, :]))(pslot, h, eff_d),
                  reads=[f"prep_st{pslot}"], writes=[f"{tag}effd_h{h}"],
                  sem=f"effout{pslot}")
        return ldop

    def emit_prep_q_blk(blk, waits=()):
        return prep_blk(wq_d, wmapq, "wmapq", weq_d, "q", blk, waits=waits)

    def emit_prep_k_blk(blk, waits=()):
        return prep_blk(wk_d, wmapk, "wmapk", wek_d, "k", blk, cd_only=True, waits=waits)

    def emit_bias_rows():
        for (bc, bres, mapw, mres, brow, brres) in (
                (bq_c, "bq_c", wmapq, "wmapq", bq_row, "bq_row"),
                (bk_c, "bk_c", wmapk, "wmapk", bk_row, "bk_row")):
            for h in range(H):
                P.comp("tensor",
                       (lambda bc, mapw, h: lambda e: e.matmul(
                           ps_ef[0:1, h * M:(h + 1) * M], bc[:, h:h + 1], mapw[:, :],
                           start=True, stop=True))(bc, mapw, h),
                       reads=[bres, mres], writes=[f"ps_ef{h // 8}"])
            P.comp("scalar",
                   (lambda brow: lambda e: e.activation(brow[:, :], ps_ef[0:1, 0:1024],
                                                        AF.Copy))(brow),
                   reads=["ps_ef0", "ps_ef1"], writes=[brres])


    xn_fm3 = r3(xn_fm, T)
    xn_tm3 = r3(xn_tm, D)
    weq3d = r3(weq_d.ap(), H * M)

    def emit_effs_load(cch):
        slot = cch % 3
        P.dma("sync",
              (lambda slot, cch: lambda e: e.dma_start(
                  out=r3(effs[slot], 128)[:, :, :],
                  in_=weq3d[:, :, cch * 128:(cch + 1) * 128]))(slot, cch),
              writes=[f"effs{slot}"], reads=[f"qeffd_h{2 * cch}", f"qeffd_h{2 * cch + 1}"],
              sem=f"effs{slot}")

    _chain = [None]
    for i in range(NT):
        slot = i % 3
        xr = f"xs{slot}"
        xop = P.dma("gpsimd",
              (lambda slot, i: lambda e: e.dma_start(out=xs[slot][:, :],
                                                     in_=x_d[i * 128:(i + 1) * 128, :]))(slot, i),
              writes=[xr], sem=f"xs{slot}",
              waits=[_chain[0]] if _chain[0] is not None else ())
        _chain[0] = emit_prep_q_blk(i, waits=[xop])
        if i in (2, 3, 4):
            emit_effs_load(i - 2)
        j = i % 2
        for c in range(4):
            P.comp("vector",
                   (lambda slot, c, j: lambda e: e.bn_stats(st24[j][:, c * 6:(c + 1) * 6],
                                                            xs[slot][:, c * 512:(c + 1) * 512]))(slot, c, j),
                   reads=[xr], writes=[f"st24_{j}"])
        P.comp("vector", (lambda j: lambda e: e.bn_aggr(mv[j][:, :], st24[j][:, :]))(j),
               reads=[f"st24_{j}"], writes=[f"mv{j}"])
        P.comp("scalar", (lambda j: lambda e: e.activation(tca[j][:, :], mv[j][:, 1:2], AF.Ln,
                                                           bias=eps1_col[:, :]))(j),
               reads=[f"mv{j}", "eps1_col"], writes=[f"tca{j}"])
        P.comp("scalar", (lambda j: lambda e: e.activation(r1c[j][:, :], tca[j][:, :], AF.Exp,
                                                           scale=nhalf_col[:, :]))(j),
               reads=[f"tca{j}", "nhalf_col"], writes=[f"r1c{j}"])
        P.comp("vector", (lambda j: lambda e: e.tensor_mul(tcb[j][:, :], r1c[j][:, :], r1c[j][:, :]))(j),
               reads=[f"r1c{j}"], writes=[f"tcb{j}"])
        P.comp("vector", (lambda j: lambda e: e.tensor_mul(tcb[j][:, :], tcb[j][:, :], mv[j][:, 1:2]))(j),
               reads=[f"mv{j}"], writes=[f"tcb{j}"])
        P.comp("scalar", (lambda j: lambda e: e.activation(tcb[j][:, :], tcb[j][:, :], AF.Ln,
                                                           bias=eps1_col[:, :]))(j),
               reads=["eps1_col"], writes=[f"tcb{j}"])
        P.comp("scalar",
               (lambda i, j: lambda e: e.activation(r2cs[:, i:i + 1], tcb[j][:, :], AF.Exp,
                                                    scale=nhalf_col[:, :]))(i, j),
               reads=[f"tcb{j}", "nhalf_col"], writes=["r2cs"])
        P.comp("vector",
               (lambda slot, i, j: lambda e: e.tensor_scalar(xn_tm3[:, i, :], xs[slot][:, :],
                                                             mv[j][:, 0:1], r1c[j][:, :],
                                                             ALU.subtract, ALU.mult))(slot, i, j),
               reads=[xr, f"mv{j}", f"r1c{j}"], writes=[f"xnt{i}"])
        for half in range(2):
            pst, pr = (ps_t1, "ps_t1") if half == 0 else (ps_t2, "ps_t2")
            for kk in range(8):
                kc = half * 8 + kk
                P.comp("tensor",
                       (lambda pst, kk, kc, i: lambda e: e.transpose(
                           pst[:, kk * 128:(kk + 1) * 128],
                           xn_tm3[:, i, kc * 128:(kc + 1) * 128], ident[:, :]))(pst, kk, kc, i),
                       reads=[f"xnt{i}", "ident"], writes=[pr])
            P.comp("vector",
                   (lambda pst, half, i: lambda e: e.tensor_copy(
                       xn_fm3[:, half * 8:(half + 1) * 8, i * 128:(i + 1) * 128],
                       r3(pst, 128)[:, :, :]))(pst, half, i),
                   reads=[pr], writes=[f"xnf{i}"])

    XNF_ALL = [f"xnf{i}" for i in range(NT)]
    XNT_ALL = [f"xnt{i}" for i in range(NT)]
    emit_bias_rows()

    qp3 = r3(qp, T)
    kp_tm3 = r3(kp_tm, 1024)

    def elu_into(dst_ap_fn, dst_res, psx, pre):
        P.comp("vector",
               (lambda psx, dst_ap_fn: lambda e: e.tensor_scalar(dst_ap_fn(), psx[:, :], 0.0, None,
                                                                 ALU.max))(psx, dst_ap_fn),
               reads=[f"{pre}0", f"{pre}1"], writes=[dst_res])
        P.comp("vector",
               (lambda psx: lambda e: e.tensor_scalar(elu1[:, :], psx[:, :], 0.0, None,
                                                      ALU.min))(psx),
               reads=[f"{pre}0", f"{pre}1"], writes=["elu1"])
        P.comp("scalar", lambda e: e.activation(elu1[:, :], elu1[:, :], AF.Exp), writes=["elu1"])
        P.comp("vector",
               (lambda dst_ap_fn: lambda e: e.tensor_add(dst_ap_fn(), dst_ap_fn(), elu1[:, :]))(dst_ap_fn),
               reads=["elu1"], writes=[dst_res])

    for cch in range(8):
        emit_prep_k_blk(cch)
        slot = cch % 3
        eres = f"effs{slot}"
        psx, pre = (ps_ab, "ps_ab") if cch % 2 == 0 else (ps_ef, "ps_ef")
        for kc in range(NK):
            for half in range(2):
                P.comp("tensor",
                       (lambda slot, kc, half, psx: lambda e: e.matmul(
                           psx[:, half * 512:(half + 1) * 512],
                           effs[slot][:, kc * 128:(kc + 1) * 128],
                           xn_fm3[:, kc, half * 512:(half + 1) * 512],
                           start=(kc == 0), stop=False))(slot, kc, half, psx),
                       reads=[eres] + XNF_ALL[half * 4:(half + 1) * 4], writes=[f"{pre}{half}"])
        for half in range(2):
            P.comp("tensor",
                   (lambda cch, half, psx: lambda e: e.matmul(
                       psx[:, half * 512:(half + 1) * 512],
                       bq_row[:, cch * 128:(cch + 1) * 128],
                       ones512_b[:, :],
                       start=False, stop=True))(cch, half, psx),
                   reads=["bq_row", "ones512_b"], writes=[f"{pre}{half}"])
        qslot = cch % 2
        elu_into((lambda qslot=qslot: qp3[:, qslot, :]), f"qpc{qslot}", psx, pre)
        P.dma("sync",
              (lambda cch, qslot: lambda e: e.dma_start(
                  out=qp_d[:, cch * T:(cch + 1) * T], in_=qp3[:, qslot, :]))(cch, qslot),
              reads=[f"qpc{qslot}"], sem=f"outq{qslot}")
        if cch + 3 < 8:
            emit_effs_load(cch + 3)

    wfa3in = wfa_d.rearrange("(c p) o -> p c o", p=128)
    P.dma("gpsimd", lambda e: e.dma_start(out=wfa[:, :], in_=wfa3in[:, 0, :]),
          writes=["wfas"], sem="wfal")
    P.comp("vector",
           lambda e: e.reduce_max(am_col[:, :], wfa[:, :], axis=AXX,
                                  apply_absolute_value=True),
           reads=["wfas"], writes=["am_col"])
    wfb_out0 = P.dma("sync", lambda e: e.dma_start(out=wfb_d[:, 0:D], in_=wfa[:, :]),
                     reads=["wfas"], sem="outa")
    woa3in = woa_d.rearrange("(c p) o -> p c o", p=128)
    P.dma("gpsimd", lambda e: e.dma_start(out=woa[:, :], in_=woa3in[:, 0, :]),
          writes=["woas"], sem="woal")
    wob_out0 = P.dma("sync", lambda e: e.dma_start(out=wob_d[:, 0:D], in_=woa[:, :]),
                     reads=["woas"], sem="outa")

    wek3d = r3(wek_d.ap(), H * M)
    for j in range(2):
        P.dma("sync",
              (lambda j: lambda e: e.dma_start(
                  out=r3(stg[j], 1024)[:, :, :],
                  in_=wek3d[:, j * 8:(j + 1) * 8, :]))(j),
              writes=[f"stg{j}"], reads=[f"keffd_h{h}" for h in range(H)], sem=f"stg{j}")
    stgk = [r3(stg[0], 1024), r3(stg[1], 1024)]
    for i in range(NT):
        psx, pre = (ps_ab, "ps_ab") if i % 2 == 0 else (ps_ef, "ps_ef")
        for kc in range(NK):
            j, kk = (0, kc) if kc < 8 else (1, kc - 8)
            for half in range(2):
                P.comp("tensor",
                       (lambda i, kc, half, psx, j, kk: lambda e: e.matmul(
                           psx[:, half * 512:(half + 1) * 512],
                           xn_fm3[:, kc, i * 128:(i + 1) * 128],
                           stgk[j][:, kk, half * 512:(half + 1) * 512],
                           start=(kc == 0), stop=False))(i, kc, half, psx, j, kk),
                       reads=[f"xnf{i}", f"stg{j}"], writes=[f"{pre}{half}"])
        for half in range(2):
            P.comp("tensor",
                   (lambda half, psx: lambda e: e.matmul(
                       psx[:, half * 512:(half + 1) * 512],
                       ones_row_bb[0:1, 0:128],
                       bk_row[:, half * 512:(half + 1) * 512],
                       start=False, stop=True))(half, psx),
                   reads=["bk_row", "ones_row_bb"], writes=[f"{pre}{half}"])
        elu_into((lambda i=i: kp_tm3[:, i, :]), f"kpt{i}", psx, pre)
    KPT_ALL = [f"kpt{i}" for i in range(NT)]

    fillers = []
    _lists = {"r2": [], "xn2": [], "ksum": [], "conv": []}
    _cur = ["r2"]

    def F(fn):
        _lists[_cur[0]].append(fn)

    for i in range(0, NT, 2):
        def r2_pair(i=i):
            for ii in (i, i + 1):
                P.comp("vector",
                       (lambda ii: lambda e: e.tensor_scalar(diag[:, :], ident[:, :],
                                                             r2cs[:, ii:ii + 1], None,
                                                             ALU.mult))(ii),
                       reads=["ident", "r2cs"], writes=["diagr"])
                P.comp("tensor",
                       (lambda ii: lambda e: e.matmul(ps_cd[0:1, ii * 128:(ii + 1) * 128],
                                                      ones_col_f[:, :], diag[:, :],
                                                      start=True, stop=True))(ii),
                       reads=["diagr", "ones_col_f"], writes=[f"ps_cd{ii // 4}"])
        F(r2_pair)

    def r2_finish():
        P.comp("scalar", lambda e: e.activation(r2row[:, :], ps_cd[0:1, 0:1024], AF.Copy),
               reads=["ps_cd0", "ps_cd1"], writes=["r2row"])
        for half in range(2):
            P.comp("tensor",
                   (lambda half: lambda e: e.matmul(
                       ps_cd[:, half * 512:(half + 1) * 512], ones_row_f[:, :],
                       r2row[:, half * 512:(half + 1) * 512], start=True, stop=True))(half),
                   reads=["r2row", "ones_row_f"], writes=[f"ps_cd{half}"])
            P.comp("vector",
                   (lambda half: lambda e: e.tensor_copy(
                       B_r2[:, half * 512:(half + 1) * 512],
                       ps_cd[:, half * 512:(half + 1) * 512]))(half),
                   reads=[f"ps_cd{half}"], writes=["B_r2"])
    F(r2_finish)



    _cur[0] = "xn2"
    for kc in range(NK):
        def xn2_pair(kc=kc):
            P.comp("vector",
                   (lambda kc: lambda e: e.tensor_mul(xn_fm3[:, kc, :], xn_fm3[:, kc, :],
                                                      B_r2[:, :]))(kc),
                   reads=["B_r2"] + XNF_ALL, writes=[f"x2f{kc}"])
            P.comp("vector",
                   (lambda kc: lambda e: e.tensor_scalar(xn_fm3[:, kc, :], xn_fm3[:, kc, :],
                                                         lnw_c[:, kc:kc + 1], lnb_c[:, kc:kc + 1],
                                                         ALU.mult, ALU.add))(kc),
                   reads=["lnw_c", "lnb_c"], writes=[f"x2f{kc}"])
        F(xn2_pair)
        if kc == 7 or kc == 15:
            def xn2_out(j=kc // 8):
                xn2_outs.append(P.dma("sync",
                                      (lambda j: lambda e: e.dma_start(
                                          out=xn2_d[:, j * 8 * T:(j + 1) * 8 * T],
                                          in_=xn_fm[:, j * 8 * T:(j + 1) * 8 * T]))(j),
                                      reads=[f"x2f{k}" for k in range(j * 8, (j + 1) * 8)],
                                      sem="outx"))
            F(xn2_out)

    _cur[0] = "ksum"
    for h in range(H):
        def ksum_head(h=h):
            for i in range(NT):
                P.comp("tensor",
                       (lambda i, h: lambda e: e.matmul(
                           ps_cd[0:64, h:h + 1],
                           kp_tm3[:, i, h * M:(h + 1) * M], ones_col_b[:, :],
                           start=(i == 0), stop=(i == NT - 1)))(i, h),
                       reads=[f"kpt{i}", "ones_col_b"], writes=["ps_cd0"])
        F(ksum_head)

    def ksum_evac():
        P.comp("vector", lambda e: e.tensor_copy(ksum_cf[:, :], ps_cd[0:64, 0:16]),
               reads=["ps_cd0"], writes=["ksum_cf"])
        P.comp("vector", lambda e: e.memset(kv_sb[:, 1024:1040], 0.0), writes=["kv_sb"])
        P.comp("vector", lambda e: e.tensor_copy(kv_sb[0:64, 1024:1040], ksum_cf[:, :]),
               reads=["ksum_cf"], writes=["kv_sb"])
    F(ksum_evac)

    _cur[0] = "conv"
    for h in range(0, H, 2):
        def conv_pair(h=h):
            for hh in (h, h + 1):
                P.comp("vector",
                       (lambda hh: lambda e: e.tensor_scalar(diag[0:64, 0:64], ident[0:64, 0:64],
                                                             ksum_cf[:, hh:hh + 1], None,
                                                             ALU.mult))(hh),
                       reads=["ident", "ksum_cf"], writes=["diagr"])
                P.comp("tensor",
                       (lambda hh: lambda e: e.matmul(ps_cd[0:1, hh * M:(hh + 1) * M],
                                                      ones_col_f[0:64, :], diag[0:64, 0:64],
                                                      start=True, stop=True))(hh),
                       reads=["diagr", "ones_col_f"], writes=[f"ps_cd{hh // 8}"])
        F(conv_pair)

    def ksum_row_evac():
        P.comp("scalar", lambda e: e.activation(ksum_rb[:, :], ps_cd[0:1, 0:1024], AF.Copy),
               reads=["ps_cd0", "ps_cd1"], writes=["ksum_rb"])
    F(ksum_row_evac)


    fillers.extend(_lists["r2"])
    xl = _lists["xn2"]
    kl = _lists["ksum"] + _lists["conv"]
    xi = ki = 0
    while xi < len(xl) or ki < len(kl):
        if xi < len(xl):
            fillers.append(xl[xi]); xi += 1
            if xi < len(xl) and xi in (8, 17):
                fillers.append(xl[xi]); xi += 1
        for _ in range(2):
            if ki < len(kl):
                fillers.append(kl[ki]); ki += 1

    xn2_outs = []
    xn_tmD = r3(xn_tm, D)
    def emit_wv_load(kc):
        slot = kc % 5
        P.dma("gpsimd",
              (lambda slot, kc: lambda e: e.dma_start(out=whs[slot][:, :],
                                                      in_=wvT_d[kc * 128:(kc + 1) * 128, :]))(slot, kc),
              writes=[f"whs{slot}"], sem=f"whs{slot}")

    for kc in range(5):
        emit_wv_load(kc)
    P.dma("gpsimd", lambda e: e.dma_start(out=wfa[:, :], in_=wfa3in[:, 1, :]),
          reads=["wfas"], writes=["wfas"], sem="wfal")
    P.comp("vector",
           lambda e: e.reduce_max(am_tmp[:, :], wfa[:, :], axis=AXX,
                                  apply_absolute_value=True),
           reads=["wfas"], writes=["am_tmp"])
    P.comp("vector", lambda e: e.tensor_max(am_col[:, :], am_col[:, :], am_tmp[:, :]),
           reads=["am_tmp"], writes=["am_col"])
    wfb_out1 = P.dma("sync", lambda e: e.dma_start(out=wfb_d[:, D:2 * D], in_=wfa[:, :]),
                     reads=["wfas"], sem="outa")
    P.dma("gpsimd", lambda e: e.dma_start(out=woa[:, :], in_=woa3in[:, 1, :]),
          reads=["woas"], writes=["woas"], sem="woal")
    for kc in range(NK):
        for _ in range(3):
            if fillers:
                fillers.pop(0)()
        slot = kc % 5
        wres = f"whs{slot}"
        psx, pre = (ps_ab, "ps_ab") if kc % 2 == 0 else (ps_ef, "ps_ef")
        for half in range(2):
            for i in range(NT):
                P.comp("tensor",
                       (lambda kc, half, i, psx: lambda e: e.matmul(
                           psx[:, half * 512:(half + 1) * 512],
                           xn_tmD[:, i, kc * 128:(kc + 1) * 128],
                           kp_tm3[:, i, half * 512:(half + 1) * 512],
                           start=(i == 0), stop=(i == NT - 1)))(kc, half, i, psx),
                       reads=[f"xnt{i}", f"kpt{i}"], writes=[f"{pre}{half}"])
        aslot = kc % 2
        P.comp("vector",
               (lambda psx, aslot: lambda e: e.tensor_copy(Ats[aslot][:, :], psx[:, :]))(psx, aslot),
               reads=[f"{pre}0", f"{pre}1"], writes=[f"Ats{aslot}"])
        kvp, kvr = (ps_ef, "ps_ef") if kc % 2 == 0 else (ps_ab, "ps_ab")
        for h in range(H):
            r0 = 0 if h < 8 else 64
            P.comp("tensor",
                   (lambda h, aslot, slot, kvp, r0: lambda e: e.matmul(
                       kvp[r0:r0 + 64, (h % 8) * 128:(h % 8) * 128 + 128],
                       Ats[aslot][:, h * M:(h + 1) * M],
                       whs[slot][:, h * 128:(h + 1) * 128],
                       start=True, stop=True))(h, aslot, slot, kvp, r0),
                   reads=[f"Ats{aslot}", wres], writes=[f"{kvr}{(h % 8) // 4}"])
        if kc == 0:
            P.comp("vector",
                   (lambda kvp: lambda e: e.tensor_copy(kv_sb[:, 0:1024], kvp[:, :]))(kvp),
                   reads=[f"{kvr}0", f"{kvr}1"], writes=["kv_sb"])
        else:
            P.comp("vector",
                   (lambda kvp: lambda e: e.tensor_add(kv_sb[:, 0:1024], kv_sb[:, 0:1024],
                                                       kvp[:, :]))(kvp),
                   reads=[f"{kvr}0", f"{kvr}1"], writes=["kv_sb"])
        if kc + 5 < NK:
            emit_wv_load(kc + 5)
    while fillers:
        fillers.pop(0)()

    for h in range(H):
        r0 = 0 if h < 8 else 64
        P.comp("tensor",
               (lambda h, r0: lambda e: e.matmul(
                   ps_cd[r0:r0 + 64, (h % 8) * 128:(h % 8) * 128 + 128],
                   ksum_rb[:, h * M:(h + 1) * M],
                   bv_r[:, h * 128:(h + 1) * 128],
                   start=True, stop=True))(h, r0),
               reads=["ksum_rb", "bv_r"], writes=[f"ps_cd{(h % 8) // 4}"])
    P.comp("vector",
           lambda e: e.tensor_add(kv_sb[:, 0:1024], kv_sb[:, 0:1024], ps_cd[:, :]),
           reads=["ps_cd0", "ps_cd1"], writes=["kv_sb"])
    t_kv_out = P.dma("gpsimd", lambda e: e.dma_start(out=kvm_d, in_=kv_sb[:, 0:1024]),
                     reads=["kv_sb"], sem="outk")
    t_ks_out = P.dma("sync", lambda e: e.dma_start(out=kvs_d, in_=kv_sb[:, 1024:1040]),
                     reads=["kv_sb"], sem="outk")

    while fillers:
        fillers.pop(0)()

    P.comp("scalar", lambda e: e.activation(am_cb[:, :], am_col[:, :], AF.Copy),
           reads=["am_col"], writes=["am_cb"])
    P.comp("tensor", lambda e: e.matmul(ps_cd[0:1, 0:128], am_cb[:, :], ident[:, :],
                                        start=True, stop=True),
           reads=["am_cb", "ident"], writes=["ps_cd0"])
    P.comp("vector", lambda e: e.reduce_max(am1[:, :], ps_cd[0:1, 0:128], axis=AXX),
           reads=["ps_cd0"], writes=["am1"])
    amax_out = P.dma("sync", lambda e: e.dma_start(out=amax_d, in_=am1[:, :]),
                     reads=["am1"], sem="outa")
    wob_out = P.dma("sync", lambda e: e.dma_start(out=wob_d[:, D:2 * D], in_=woa[:, :]),
                    reads=["woas"], sem="outa")

    dbg_outs = []
    if DEBUG:
        dbg_outs.append(P.dma("sync", lambda e: e.dma_start(out=kptm_dbg, in_=kp_tm[:, :]),
                              reads=KPT_ALL, sem="outdbg"))
        dbg_outs.append(P.dma("sync", lambda e: e.dma_start(out=stats_dbg, in_=r2cs[:, :]),
                              reads=["r2cs"], sem="outdbg"))
    qp_outs = [op for op in P.ops if op.dma and op.stream and op.stream.startswith('outq')]
    P.comp("sync", lambda e: e.nop(),
           waits=[t_kv_out, t_ks_out, amax_out, wfb_out0, wfb_out1, wob_out0, wob_out] + qp_outs + xn2_outs + dbg_outs)

    P.emit(ctx)
    return nc



def build_launch2():
    nc = bass.Bass("TRN2", target_bir_lowering=False, detect_race_conditions=False)
    ctx = ExitStack()

    kvallm_d = nc.dram_tensor("kvallm", [NCORES * 128, 1024], BF16, kind="ExternalInput").ap()
    kvalls_d = nc.dram_tensor("kvalls", [NCORES * 128, 16], F32, kind="ExternalInput").ap()
    qp_d = nc.dram_tensor("qp", [128, 8 * T], BF16, kind="ExternalInput").ap()
    xn2_d = nc.dram_tensor("xn2", [128, NK * T], BF16, kind="ExternalInput").ap()
    x_d = nc.dram_tensor("x", [T, D], F32, kind="ExternalInput").ap()
    wfb_d = nc.dram_tensor("wfb", [128, NK * D], BF16, kind="ExternalInput").ap()
    wob_d = nc.dram_tensor("wob", [128, NK * D], BF16, kind="ExternalInput").ap()
    bo_r_d = nc.dram_tensor("bo_r", [1, D], F32, kind="ExternalInput").ap()
    gamma_r_d = nc.dram_tensor("gamma_r", [1, D], F32, kind="ExternalInput").ap()
    beta_r_d = nc.dram_tensor("beta_r", [1, D], F32, kind="ExternalInput").ap()
    amax8_d = nc.dram_tensor("amax8", [1, NCORES], F32, kind="ExternalInput").ap()
    out_d = nc.dram_tensor("out", [T, D], F32, kind="ExternalOutput").ap()

    A = nc.alloc_sbuf_tensor
    ones_row_f = A("ones_row_f", [1, 128], F32).ap()
    ones_row_b2 = A("ones_row_b2", [1, 128], BF16).ap()
    nEPS = A("nEPS", [1, 1], F32).ap()
    nm1 = A("nm1", [1, 1], F32).ap()
    kv_acc = A("kv_acc", [128, 1040], F32).ap()
    kvbig = A("kvbig", [128, 4 * 1024], BF16).ap()
    ksbig = A("ksbig", [128, NCORES * 16], F32).ap()
    kvb = A("kvb", [128, 2048], BF16).ap()
    ksum_cb = A("ksum_cb", [128, H], BF16).ap()
    qp = A("qps", [128, 8 * T], BF16).ap()
    xn2 = A("xn2s", [128, NK * T], BF16).ap()
    att = A("atts", [128, NK * T], BF16).ap()
    gob = [A(f"gob{j}", [128, NT * 512], BF16).ap() for j in range(2)]
    wsg = [A(f"wsg{j}", [128, NK * 512], BF16).ap() for j in range(2)]
    B_gs = A("B_gs", [128, D], BF16).ap()
    B_beta = A("B_beta", [128, D], BF16).ap()
    B_bo = A("B_bo", [128, D], BF16).ap()
    fsc = [A(f"fsc{j}", [128, 512], F32).ap() for j in range(2)]
    gamma_r = A("gamma_rs", [1, D], BF16).ap()
    beta_r = A("beta_rs", [1, D], BF16).ap()
    bo_r = A("bo_rs", [1, D], BF16).ap()
    am8 = A("am8", [1, NCORES], F32).ap()
    am_sc = A("am_sc", [1, 1], F32).ap()
    scale_col = A("scale_col", [128, 1], F32).ap()
    lnr = [A(f"lnr{j}", [1, T], F32).ap() for j in range(2)]
    lnrb = [A(f"lnrb{j}", [1, T], BF16).ap() for j in range(2)]
    rbs = [A(f"rbs{j}", [128, T], F32).ap() for j in range(1)]
    xsb = [A(f"xsb{j}", [128, 512], F32).ap() for j in range(2)]
    osc = [A(f"osc{j}", [128, 512], F32).ap() for j in range(2)]

    APs = nc.alloc_psum_tensor
    ps_a = APs("ps_a", [128, 1024], F32).ap()
    ps_b = APs("ps_b", [128, 1024], F32).ap()
    ps_s = APs("ps_s", [128, 1024], F32).ap()
    ps_n = APs("ps_n", [1, 1024], F32).ap()

    P = Prog(nc)

    wob3 = r3(wob_d, D)
    wfb3 = r3(wfb_d, D)
    qp3 = r3(qp, T)
    xn23 = r3(xn2, T)
    att3 = r3(att, T)

    P.comp("vector", lambda e: e.memset(ones_row_f[:, :], 1.0), writes=["ones_row_f"])
    P.comp("vector", lambda e: e.memset(ones_row_b2[:, :], 1.0), writes=["ones_row_b2"])
    P.comp("vector", lambda e: e.memset(nEPS[:, :], EPS2), writes=["nEPS"])
    P.comp("vector", lambda e: e.memset(nm1[:, :], -1.0), writes=["nm1"])

    P.dma("gpsimd", lambda e: e.dma_start(out=bo_r[:, :], in_=bo_r_d), writes=["bo_r"], sem="csts", barrier=True)
    P.dma("gpsimd", lambda e: e.dma_start(out=gamma_r[:, :], in_=gamma_r_d), writes=["gamma_r"], sem="csts", barrier=True)
    P.dma("gpsimd", lambda e: e.dma_start(out=beta_r[:, :], in_=beta_r_d), writes=["beta_r"], sem="csts", barrier=True)
    t_wf0 = P.dma("gpsimd", lambda e: e.dma_start(out=r3(wsg[0], 512)[:, :, :],
                                                  in_=wfb3[:, :, 0:512]),
                  writes=["wsg0"], sem="wsg0")
    t_xn2 = P.dma("scalar", lambda e: e.dma_start(out=xn2[:, :], in_=xn2_d), writes=["xn2l"], sem="xnl")
    P.dma("scalar", lambda e: e.dma_start(out=qp[:, :], in_=qp_d), writes=["qp"], sem="qpl",
          waits=[t_wf0])

    P.dma("sync", lambda e: e.dma_start(out=am8[:, :], in_=amax8_d), writes=["am8"], sem="aml")
    P.comp("vector", lambda e: e.reduce_max(am_sc[:, :], am8[:, :], axis=AXX),
           reads=["am8"], writes=["am_sc"])
    P.comp("tensor", lambda e: e.matmul(ps_b[0:128, 0:1], ones_row_f[:, :], am_sc[:, :],
                                        start=True, stop=True),
           reads=["am_sc", "ones_row_f"], writes=["ps_b0"])
    P.comp("vector", lambda e: e.tensor_copy(scale_col[:, :], ps_b[0:128, 0:1]),
           reads=["ps_b0"], writes=["scale_col"])

    for q in range(4):
        bank = q % 2
        P.comp("tensor",
               (lambda q, bank: lambda e: e.matmul(
                   ps_b[:, bank * 512:(bank + 1) * 512], ones_row_b2[:, :],
                   bo_r[:, q * 512:(q + 1) * 512], start=True, stop=True))(q, bank),
               reads=["bo_r", "ones_row_b2"], writes=[f"ps_b{bank}"])
        P.comp("vector",
               (lambda q, bank: lambda e: e.tensor_copy(
                   B_bo[:, q * 512:(q + 1) * 512],
                   ps_b[:, bank * 512:(bank + 1) * 512]))(q, bank),
               reads=[f"ps_b{bank}"], writes=["B_bo"])
    for q in range(4):
        bank = q % 2
        P.comp("tensor",
               (lambda q, bank: lambda e: e.matmul(
                   ps_b[:, bank * 512:(bank + 1) * 512], ones_row_b2[:, :],
                   gamma_r[:, q * 512:(q + 1) * 512], start=True, stop=True))(q, bank),
               reads=["gamma_r", "ones_row_b2"], writes=[f"ps_b{bank}"])
        P.comp("vector",
               (lambda q, bank: lambda e: e.tensor_scalar(
                   B_gs[:, q * 512:(q + 1) * 512],
                   ps_b[:, bank * 512:(bank + 1) * 512],
                   scale_col[:, :], None, ALU.mult))(q, bank),
               reads=[f"ps_b{bank}", "scale_col"], writes=["B_gs"])
    for q in range(4):
        bank = q % 2
        P.comp("tensor",
               (lambda q, bank: lambda e: e.matmul(
                   ps_b[:, bank * 512:(bank + 1) * 512], ones_row_b2[:, :],
                   beta_r[:, q * 512:(q + 1) * 512], start=True, stop=True))(q, bank),
               reads=["beta_r", "ones_row_b2"], writes=[f"ps_b{bank}"])
        P.comp("vector",
               (lambda q, bank: lambda e: e.tensor_copy(
                   B_beta[:, q * 512:(q + 1) * 512],
                   ps_b[:, bank * 512:(bank + 1) * 512]))(q, bank),
               reads=[f"ps_b{bank}"], writes=["B_beta"])

    kvm3 = kvallm_d.rearrange("(c p) o -> p c o", p=128)
    kvb3 = r3(kvbig, 1024)
    t_kvm1 = P.dma("sync", lambda e: e.dma_start(out=kvb3[:, :, :], in_=kvm3[:, 0:4, :]),
                   writes=["kvbig"], sem="kva", waits=[t_wf0, t_xn2])
    P.dma("sync", lambda e: e.dma_start(out=r3(ksbig, 16)[:, :, :],
                                        in_=kvalls_d.rearrange("(c p) o -> p c o", p=128)),
          writes=["ksbig"], sem="ksl")
    P.comp("gpsimd", lambda e: e.tensor_copy(kv_acc[:, 0:1024], kvb3[:, 0, :]),
           reads=["kvbig"], writes=["kv_acc"])
    for c in range(1, 4):
        P.comp("gpsimd",
               (lambda c: lambda e: e.tensor_add(kv_acc[:, 0:1024], kv_acc[:, 0:1024],
                                                 kvb3[:, c, :]))(c),
               reads=["kvbig"], writes=["kv_acc"])
    P.dma("sync", lambda e: e.dma_start(out=kvb3[:, :, :], in_=kvm3[:, 4:8, :]),
          writes=["kvbig"], sem="kva")
    for c in range(4):
        P.comp("gpsimd",
               (lambda c: lambda e: e.tensor_add(kv_acc[:, 0:1024], kv_acc[:, 0:1024],
                                                 kvb3[:, c, :]))(c),
               reads=["kvbig"], writes=["kv_acc"])
    P.comp("gpsimd", lambda e: e.tensor_copy(kv_acc[:, 1024:1040], r3(ksbig, 16)[:, 0, :]),
           reads=["ksbig"], writes=["kv_acc"])
    for c in range(1, NCORES):
        P.comp("gpsimd",
               (lambda c: lambda e: e.tensor_add(kv_acc[:, 1024:1040], kv_acc[:, 1024:1040],
                                                 r3(ksbig, 16)[:, c, :]))(c),
               reads=["ksbig"], writes=["kv_acc"])
    P.comp("gpsimd", lambda e: e.tensor_copy(ksum_cb[0:64, :], kv_acc[0:64, 1024:1040]),
           reads=["kv_acc"], writes=["ksum_cb"])
    P.dma("sync", lambda e: e.dma_start(out=ksum_cb[64:128, :], in_=ksum_cb[0:64, :]),
          reads=["ksum_cb"], writes=["ksum_cb"], sem="ksd")
    ka3 = r3(kv_acc[:, 0:1024], 128)
    kv3 = r3(kvb, 128)
    P.dma("gpsimd", lambda e: e.dma_start(out=kv3[0:64, 0:8:2, :], in_=ka3[0:64, 0:8:2, :]),
          reads=["kv_acc"], writes=["kvb"], sem="dup")
    P.dma("gpsimd", lambda e: e.dma_start(out=kv3[64:128, 1:8:2, :], in_=ka3[0:64, 1:8:2, :]),
          reads=["kv_acc"], writes=["kvb"], sem="dup")
    P.dma("gpsimd", lambda e: e.dma_start(out=kv3[0:64, 8:16:2, :], in_=ka3[64:128, 0:8:2, :]),
          reads=["kv_acc"], writes=["kvb"], sem="dup")
    P.dma("gpsimd", lambda e: e.dma_start(out=kv3[64:128, 9:16:2, :], in_=ka3[64:128, 1:8:2, :]),
          reads=["kv_acc"], writes=["kvb"], sem="dup")
    P.dma("gpsimd", lambda e: e.dma_start(out=r3(wsg[1], 512)[:, :, :],
                                          in_=wob3[:, :, 0:512]),
          writes=["wsg1"], sem="wsg1")

    def emit_ffn_block(ob):
        if ob > 0:
            P.dma("gpsimd",
                  (lambda ob: lambda e: e.dma_start(
                      out=r3(wsg[0], 512)[:, :, :],
                      in_=wfb3[:, :, ob * 512:(ob + 1) * 512]))(ob),
                  writes=["wsg0"], sem="wsg0")
        P.comp("scalar", lambda e: e.activation(wsg[0][:, :], wsg[0][:, :], AF.Sign),
               writes=["wsg0"])
        gslot = ob % 2
        for i in range(NT):
            bank = i % 2
            pv = f"ps_s{bank}"
            for kc in range(NK):
                P.comp("tensor",
                       (lambda i, kc, bank: lambda e: e.matmul(
                           ps_s[:, bank * 512:(bank + 1) * 512],
                           xn23[:, kc, i * 128:(i + 1) * 128],
                           r3(wsg[0], 512)[:, kc, :],
                           start=(kc == 0), stop=(kc == NK - 1)))(i, kc, bank),
                       reads=["xn2l", "wsg0"], writes=[pv])
            P.comp("vector",
                   (lambda i, ob, bank: lambda e: e.tensor_mul(
                       fsc[bank][:, :], ps_s[:, bank * 512:(bank + 1) * 512],
                       B_gs[:, ob * 512:(ob + 1) * 512]))(i, ob, bank),
                   reads=[pv, "B_gs"], writes=[f"fsc{bank}"])
            P.comp("vector",
                   (lambda ob, bank: lambda e: e.tensor_add(
                       fsc[bank][:, :], fsc[bank][:, :],
                       B_beta[:, ob * 512:(ob + 1) * 512]))(ob, bank),
                   reads=["B_beta"], writes=[f"fsc{bank}"])
            P.comp("scalar",
                   (lambda i, bank, gslot: lambda e: e.activation(
                       gob[gslot][:, i * 512:(i + 1) * 512], fsc[bank][:, :],
                       AF.Silu))(i, bank, gslot),
                   reads=[f"fsc{bank}"], writes=[f"gob{gslot}"])

    def emit_attention():
        def emit_norm(h):
            p0 = (h % 2) * 64
            ch = h // 2
            hs = h % 2
            for half in range(2):
                P.comp("tensor",
                       (lambda h, half, p0, ch: lambda e: e.matmul(
                           ps_n[0:1, half * 512:(half + 1) * 512],
                           ksum_cb[p0:p0 + 64, h:h + 1],
                           qp3[p0:p0 + 64, ch, half * 512:(half + 1) * 512],
                           start=True, stop=True))(h, half, p0, ch),
                       reads=["ksum_cb", "qp"], writes=[f"ps_n{half}"])
            P.comp("scalar",
                   (lambda hs: lambda e: e.activation(
                       lnr[hs][:, :], ps_n[0:1, :], AF.Ln,
                       bias=nEPS[:, :]))(hs),
                   reads=["ps_n0", "ps_n1", "nEPS"], writes=[f"lnr{hs}"])
            P.comp("scalar",
                   (lambda hs: lambda e: e.activation(lnrb[hs][:, :], lnr[hs][:, :], AF.Exp,
                                                      scale=nm1[:, :]))(hs),
                   reads=["nm1", f"lnr{hs}"], writes=[f"lnrb{hs}"])

        def emit_apply(h):
            p0 = (h % 2) * 64
            ch = h // 2
            hs = h % 2
            for half in range(2):
                P.comp("tensor",
                       (lambda half, hs: lambda e: e.matmul(
                           ps_b[:, half * 512:(half + 1) * 512], ones_row_b2[:, :],
                           lnrb[hs][:, half * 512:(half + 1) * 512],
                           start=True, stop=True))(half, hs),
                       reads=[f"lnrb{hs}", "ones_row_b2"], writes=[f"ps_b{half}"])
            P.comp("vector",
                   lambda e: e.tensor_copy(rbs[0][:, :], ps_b[:, :]),
                   reads=["ps_b0", "ps_b1"], writes=["rbs0"])
            for half in range(2):
                P.comp("tensor",
                       (lambda h, half, p0, ch: lambda e: e.matmul(
                           ps_a[:, half * 512:(half + 1) * 512],
                           kvb[p0:p0 + 64, h * 128:(h + 1) * 128],
                           qp3[p0:p0 + 64, ch, half * 512:(half + 1) * 512],
                           start=True, stop=True))(h, half, p0, ch),
                       reads=["kvb", "qp"], writes=[f"ps_a{half}"])
            P.comp("vector",
                   (lambda h: lambda e: e.tensor_mul(
                       att3[:, h, :], ps_a[:, :], rbs[0][:, :]))(h),
                   reads=["ps_a0", "ps_a1", "rbs0"], writes=["att"])

        for h in range(H + 1):
            if h < H:
                emit_norm(h)
            if h >= 1:
                emit_apply(h - 1)

    out_dmas = []

    def emit_oproj_block(ob):
        if ob > 0:
            P.dma("gpsimd",
                  (lambda ob: lambda e: e.dma_start(
                      out=r3(wsg[1], 512)[:, :, :],
                      in_=wob3[:, :, ob * 512:(ob + 1) * 512]))(ob),
                  writes=["wsg1"], sem="wsg1")
        gslot = ob % 2
        for i in range(NT):
            bank = i % 2
            pv = f"ps_s{bank}"
            xslot = i % 2
            P.dma("sync",
                  (lambda i, ob, xslot: lambda e: e.dma_start(
                      out=xsb[xslot][:, :],
                      in_=x_d[i * 128:(i + 1) * 128, ob * 512:(ob + 1) * 512]))(i, ob, xslot),
                  writes=[f"xsb{xslot}"], sem=f"xsb{xslot}")
            for kc in range(NK):
                P.comp("tensor",
                       (lambda i, ob, kc, bank: lambda e: e.matmul(
                           ps_s[:, bank * 512:(bank + 1) * 512],
                           att3[:, kc, i * 128:(i + 1) * 128],
                           r3(wsg[1], 512)[:, kc, :],
                           start=(kc == 0), stop=(kc == NK - 1)))(i, ob, kc, bank),
                       reads=["att", "wsg1"], writes=[pv])
            P.comp("vector",
                   (lambda ob, bank: lambda e: e.tensor_add(
                       osc[bank][:, :], ps_s[:, bank * 512:(bank + 1) * 512],
                       B_bo[:, ob * 512:(ob + 1) * 512]))(ob, bank),
                   reads=[pv, "B_bo"], writes=[f"osc{bank}"])
            P.comp("vector",
                   (lambda i, bank, gslot: lambda e: e.tensor_mul(
                       osc[bank][:, :], osc[bank][:, :],
                       gob[gslot][:, i * 512:(i + 1) * 512]))(i, bank, gslot),
                   reads=[f"gob{gslot}"], writes=[f"osc{bank}"])
            P.comp("vector",
                   (lambda bank, xslot: lambda e: e.tensor_add(
                       osc[bank][:, :], osc[bank][:, :],
                       xsb[xslot][:, :]))(bank, xslot),
                   reads=[f"xsb{xslot}"], writes=[f"osc{bank}"])
            out_dmas.append(P.dma("sync",
                                  (lambda i, ob, bank: lambda e: e.dma_start(
                                      out=out_d[i * 128:(i + 1) * 128, ob * 512:(ob + 1) * 512],
                                      in_=osc[bank][:, :]))(i, ob, bank),
                                  reads=[f"osc{bank}"], sem=f"outo{bank}"))

    emit_ffn_block(0)
    emit_attention()
    emit_oproj_block(0)
    for ob in range(1, 4):
        emit_ffn_block(ob)
        emit_oproj_block(ob)

    P.comp("sync", lambda e: e.nop(), waits=out_dmas)

    P.emit(ctx)
    return nc



_NC1 = None
_NC2 = None


def kernel(**inputs):
    global _NC1, _NC2
    x_full = np.asarray(inputs["L0_attn_hidden_states"])
    wq = np.asarray(inputs["L0_attn_wq"])
    wk = np.asarray(inputs["L0_attn_wk"])
    wv = np.asarray(inputs["L0_attn_wv"])
    wo = np.asarray(inputs["L0_attn_wo"])
    bq = np.asarray(inputs["L0_attn_bq"])
    bk = np.asarray(inputs["L0_attn_bk"])
    bv = np.asarray(inputs["L0_attn_bv"])
    bo = np.asarray(inputs["L0_attn_bo"])
    wmq = np.asarray(inputs["L0_attn_wmap_q"])
    wmk = np.asarray(inputs["L0_attn_wmap_k"])
    wf = np.asarray(inputs["L0_ffn_weight"])
    gamma = np.asarray(inputs["L0_ffn_gamma"])
    beta = np.asarray(inputs["L0_ffn_beta"])
    lnw = np.asarray(inputs["L0_ffn_ln_weight"])
    lnb = np.asarray(inputs["L0_ffn_ln_bias"])

    C = np.ascontiguousarray
    wvT = C(wv.T)
    woT = C(wo.T)
    wfT = C(wf.T)
    wmapqT = C(wmq.T)
    wmapkT = C(wmk.T)
    bq_c = C(bq.reshape(H, DH).T)
    bk_c = C(bk.reshape(H, DH).T)
    lnw_c = C(lnw.reshape(NK, 128).T)
    lnb_c = C(lnb.reshape(NK, 128).T)
    bv_r = C(bv.reshape(1, D))
    bo_r = C(bo.reshape(1, D))
    gamma_r = C(gamma.reshape(1, D))
    beta_r = C(beta.reshape(1, D))
    ident = np.eye(128, dtype=BF)

    xt = x_full.reshape(8192, D)

    if _NC1 is None:
        _NC1 = build_launch1()
    in_maps1 = []
    for c in range(NCORES):
        in_maps1.append({
            "x": C(xt[c * T:(c + 1) * T, :]),
            "wq": wq, "wk": wk, "wvT": wvT,
            "wfa": C(wfT[c * 256:(c + 1) * 256, :]),
            "woa": C(woT[c * 256:(c + 1) * 256, :]),
            "wmapqT": wmapqT, "wmapkT": wmapkT,
            "bq_c": bq_c, "bk_c": bk_c, "bv_r": bv_r,
            "lnw_c": lnw_c, "lnb_c": lnb_c,
            "ident": ident,
        })
    res1 = run_bass_kernel_spmd(_NC1, in_maps1, core_ids=list(range(NCORES)))

    kvallm = np.concatenate([res1.results[c]["kvm"] for c in range(NCORES)], axis=0)
    kvalls = np.concatenate([res1.results[c]["kvs"] for c in range(NCORES)], axis=0)
    amax8 = np.concatenate([res1.results[c]["amax"] for c in range(NCORES)], axis=1)
    wfb = np.concatenate([res1.results[c]["wfb"] for c in range(NCORES)], axis=1)
    wob = np.concatenate([res1.results[c]["wob"] for c in range(NCORES)], axis=1)

    if _NC2 is None:
        _NC2 = build_launch2()
    in_maps2 = []
    for c in range(NCORES):
        in_maps2.append({
            "kvallm": kvallm,
            "kvalls": kvalls,
            "qp": res1.results[c]["qp"],
            "xn2": res1.results[c]["xn2"],
            "amax8": amax8,
            "x": C(xt[c * T:(c + 1) * T, :]),
            "wfb": wfb, "wob": wob, "bo_r": bo_r,
            "gamma_r": gamma_r, "beta_r": beta_r,
        })
    res2 = run_bass_kernel_spmd(_NC2, in_maps2, core_ids=list(range(NCORES)))

    out = np.concatenate([res2.results[c]["out"] for c in range(NCORES)], axis=0)
    return out.reshape(1, 8192, D)


# version 35
# speedup vs baseline: 1.0309x; 1.0309x over previous
import numpy as np
import ml_dtypes
from contextlib import ExitStack

import concourse.bass as bass
import concourse.mybir as mybir
from concourse.bass_utils import run_bass_kernel_spmd

F32 = mybir.dt.float32
BF16 = mybir.dt.bfloat16
AF = mybir.ActivationFunctionType
ALU = mybir.AluOpType
AXX = mybir.AxisListType.X

NCORES = 8
T = 1024
NT = 8
D = 2048
NK = 16
H = 16
DH = 128
M = 64
EPS1 = 1e-5
EPS2 = 1e-6
BF = ml_dtypes.bfloat16

ENG_SEM = {"tensor": "pe", "vector": "dve", "scalar": "act", "gpsimd": "pool", "sync": "spc"}
PER_DMA_SEMS = False
STREAM_EXCLUDE = set()


class Op:
    __slots__ = ("eng", "fn", "dma", "deps", "sem", "val", "marked", "stream", "idx")

    def __init__(self, eng, fn, dma, deps, stream=None, idx=0):
        self.eng, self.fn, self.dma, self.deps = eng, fn, dma, deps
        self.sem = None
        self.val = None
        self.marked = False
        self.stream = stream
        self.idx = idx


class Prog:

    def __init__(self, nc):
        self.nc = nc
        self.ops = []
        self.res = {}
        self.streams = {}
        self.ndma = 0

    def _edges(self, reads, writes, waits):
        deps = set()
        for o in waits:
            if o is not None:
                deps.add(o)
        for rn in reads:
            r = self.res.setdefault(rn, {"w": None, "r": []})
            if r["w"] is not None:
                deps.add(r["w"])
        for wn in writes:
            r = self.res.setdefault(wn, {"w": None, "r": []})
            if r["w"] is not None:
                deps.add(r["w"])
            for o in r["r"]:
                deps.add(o)
        return deps

    def _add(self, eng, fn, dma, reads, writes, waits, stream=None):
        op = Op(eng, fn, dma, self._edges(reads, writes, waits), stream, len(self.ops))
        self.ops.append(op)
        for rn in reads:
            self.res[rn]["r"].append(op)
        for wn in writes:
            self.res[wn]["w"] = op
            self.res[wn]["r"] = []
        return op

    def comp(self, eng, fn, reads=(), writes=(), waits=()):
        return self._add(eng, fn, False, reads, writes, waits)

    def dma(self, queue, fn, reads=(), writes=(), waits=(), sem=None, barrier=False):
        if sem is None:
            sem = f"u{self.ndma}"
            self.ndma += 1
        st = self.streams.setdefault(sem, {"count": 0, "barrier": barrier})
        st["count"] += 16
        op = self._add(queue, fn, True, reads, writes, waits, stream=sem)
        op.val = st["count"]
        return op

    def emit(self, ctx):
        nc = self.nc
        def skip_edge(dep, op):
            return (not dep.dma) and (not op.dma) and dep.eng == op.eng == "tensor"

        for op in self.ops:
            best = {}
            for dep in op.deps:
                if skip_edge(dep, op):
                    continue
                key = ("s", dep.stream) if dep.dma else ("e", dep.eng)
                cur = best.get(key)
                if cur is None or dep.idx > cur.idx:
                    best[key] = dep
            op.deps = set(best.values())
            for dep in op.deps:
                dep.marked = True
        counters = {}
        handles = {}

        def get_sem(name):
            if name not in handles:
                handles[name] = ctx.enter_context(nc.semaphore(name))
                counters[name] = 0
            return name

        for op in self.ops:
            if op.dma:
                op.sem = get_sem(op.stream)
            elif op.marked:
                name = get_sem(ENG_SEM[op.eng])
                counters[name] += 1
                op.sem = name
                op.val = counters[name]

        with nc.Block() as block:
            for engname in ("tensor", "vector", "scalar", "gpsimd", "sync"):
                deco = getattr(block, engname)

                def make(engname=engname):
                    def body(e):
                        waited = {}
                        for op in self.ops:
                            if op.eng != engname:
                                continue
                            wants = {}
                            for dep in op.deps:
                                if skip_edge(dep, op):
                                    continue
                                assert dep.sem is not None
                                v = dep.val
                                if (dep.dma and dep.stream is not None
                                        and self.streams[dep.stream]["barrier"]):
                                    v = self.streams[dep.stream]["count"]
                                if wants.get(dep.sem, 0) < v:
                                    wants[dep.sem] = v
                            for s in sorted(wants):
                                v = wants[s]
                                if waited.get(s, 0) >= v:
                                    continue
                                e.wait_ge(handles[s], v)
                                waited[s] = v
                            inst = op.fn(e)
                            if op.sem is not None:
                                inst.then_inc(handles[op.sem], 16 if op.dma else 1)
                    return body

                deco(make())


def r3(ap, inner):
    return ap.rearrange("p (a b) -> p a b", b=inner)


DEBUG = False


def build_launch1():
    nc = bass.Bass("TRN2", target_bir_lowering=False, detect_race_conditions=False)
    ctx = ExitStack()

    x_d = nc.dram_tensor("x", [T, D], F32, kind="ExternalInput").ap()
    wq_d = nc.dram_tensor("wq", [D, D], F32, kind="ExternalInput").ap()
    wk_d = nc.dram_tensor("wk", [D, D], F32, kind="ExternalInput").ap()
    wvT_d = nc.dram_tensor("wvT", [D, D], F32, kind="ExternalInput").ap()
    wfa_d = nc.dram_tensor("wfa", [2 * 128, D], F32, kind="ExternalInput").ap()
    woa_d = nc.dram_tensor("woa", [2 * 128, D], F32, kind="ExternalInput").ap()
    wmapqT_d = nc.dram_tensor("wmapqT", [DH, M], F32, kind="ExternalInput").ap()
    wmapkT_d = nc.dram_tensor("wmapkT", [DH, M], F32, kind="ExternalInput").ap()
    bq_c_d = nc.dram_tensor("bq_c", [DH, H], F32, kind="ExternalInput").ap()
    bk_c_d = nc.dram_tensor("bk_c", [DH, H], F32, kind="ExternalInput").ap()
    bv_r_d = nc.dram_tensor("bv_r", [1, D], F32, kind="ExternalInput").ap()
    lnw_c_d = nc.dram_tensor("lnw_c", [DH, NK], F32, kind="ExternalInput").ap()
    lnb_c_d = nc.dram_tensor("lnb_c", [DH, NK], F32, kind="ExternalInput").ap()
    ident_d = nc.dram_tensor("ident", [128, 128], BF16, kind="ExternalInput").ap()

    qp_d = nc.dram_tensor("qp", [128, 8 * T], BF16, kind="ExternalOutput").ap()
    xn2_d = nc.dram_tensor("xn2", [128, NK * T], BF16, kind="ExternalOutput").ap()
    kvm_d = nc.dram_tensor("kvm", [128, 1024], BF16, kind="ExternalOutput").ap()
    kvs_d = nc.dram_tensor("kvs", [128, 16], F32, kind="ExternalOutput").ap()
    amax_d = nc.dram_tensor("amax", [1, 1], F32, kind="ExternalOutput").ap()
    wfb_d = nc.dram_tensor("wfb", [128, 2 * D], BF16, kind="ExternalOutput").ap()
    wob_d = nc.dram_tensor("wob", [128, 2 * D], BF16, kind="ExternalOutput").ap()

    weq_d = nc.dram_tensor("weq", [128, NK * H * M], BF16, kind="Internal")
    wek_d = nc.dram_tensor("wek", [128, NK * H * M], BF16, kind="Internal")
    if DEBUG:
        kptm_dbg = nc.dram_tensor("kptm_dbg", [128, NT * 1024], BF16, kind="ExternalOutput").ap()
        stats_dbg = nc.dram_tensor("stats_dbg", [128, NT], F32, kind="ExternalOutput").ap()

    A = nc.alloc_sbuf_tensor
    ident = A("identt", [128, 128], BF16).ap()
    ones_col_b = A("ones_col_b", [128, 1], BF16).ap()
    ones_col_f = A("ones_col_f", [128, 1], F32).ap()
    ones_row_f = A("ones_row_f", [1, 128], F32).ap()
    ones_row_bb = A("ones_row_bb", [1, 128], BF16).ap()
    ones512_b = A("ones512_b", [1, 512], BF16).ap()
    eps1_col = A("eps1_col", [128, 1], F32).ap()
    nhalf_col = A("nhalf_col", [128, 1], F32).ap()
    wmapq = A("wmapq", [DH, M], BF16).ap()
    wmapk = A("wmapk", [DH, M], BF16).ap()
    bq_c = A("bq_cs", [DH, H], BF16).ap()
    bk_c = A("bk_cs", [DH, H], BF16).ap()
    bv_r = A("bv_rs", [1, D], BF16).ap()
    lnw_c = A("lnw_cs", [DH, NK], F32).ap()
    lnb_c = A("lnb_cs", [DH, NK], F32).ap()

    xs = [A(f"xs{j}", [128, D], BF16).ap() for j in range(3)]
    wfa = A("wfas", [128, D], BF16).ap()
    am_tmp = A("am_tmp", [128, 1], F32).ap()
    woa = A("woas", [128, D], BF16).ap()
    am_col = A("am_col", [128, 1], F32).ap()
    am_cb = A("am_cb", [128, 1], BF16).ap()
    am1 = A("am1", [1, 1], F32).ap()
    xn_tm = A("xn_tm", [128, NT * D], BF16).ap()
    whs = [A(f"whs{j}", [128, D], BF16).ap() for j in range(5)]
    stg = [A(f"stg{j}", [128, NK * 512], BF16).ap() for j in range(2)]
    effs = [A(f"effs{j}", [128, NK * 128], BF16).ap() for j in range(3)]

    st24 = [A(f"st24_{j}", [128, 24], F32).ap() for j in range(2)]
    mv = [A(f"mv{j}", [128, 2], F32).ap() for j in range(2)]
    r1c = [A(f"r1c{j}", [128, 1], F32).ap() for j in range(2)]
    tca = [A(f"tca{j}", [128, 1], F32).ap() for j in range(2)]
    tcb = [A(f"tcb{j}", [128, 1], F32).ap() for j in range(2)]
    r2cs = A("r2cs", [128, NT], F32).ap()

    xn_fm = A("xn_fm", [128, NK * T], BF16).ap()
    elu1 = A("elu1", [128, T], BF16).ap()
    qp = A("qps", [128, 2 * T], BF16).ap()
    kp_tm = A("kp_tm", [128, NT * 1024], BF16).ap()
    Ats = [A(f"Ats{j}", [128, 1024], BF16).ap() for j in range(2)]
    ksum_cf = A("ksum_cf", [M, H], F32).ap()
    ksum_rb = A("ksum_rb", [1, H * M], BF16).ap()
    kv_sb = A("kv_sbs", [128, 1040], F32).ap()
    diag = A("diagt", [128, 128], F32).ap()
    bq_row = A("bq_row", [1, H * M], BF16).ap()
    bk_row = A("bk_row", [1, H * M], BF16).ap()
    padq = [A(f"padq{j}", [128, 128], BF16).ap() for j in range(1)]

    B_r2 = A("B_r2", [128, T], F32).ap()
    r2row = A("r2row", [1, T], F32).ap()
    prep_st = [A(f"prep_st{j}", [128, NK * M], BF16).ap() for j in range(2)]

    APs = nc.alloc_psum_tensor
    ps_ab = APs("ps_ab", [128, 1024], F32).ap()
    ps_cd = APs("ps_cd", [128, 1024], F32).ap()
    ps_ef = APs("ps_ef", [128, 1024], F32).ap()
    ps_t1 = APs("ps_t1", [128, 1024], BF16).ap()
    ps_t2 = APs("ps_t2", [128, 1024], BF16).ap()

    P = Prog(nc)

    def cdma(dst, src):
        return lambda e: e.dma_start(out=dst, in_=src)

    P.dma("gpsimd", cdma(ident[:, :], ident_d), writes=["ident"], sem="csts", barrier=True)
    P.dma("gpsimd", cdma(wmapq[:, :], wmapqT_d), writes=["wmapq"], sem="csts", barrier=True)
    P.dma("gpsimd", cdma(wmapk[:, :], wmapkT_d), writes=["wmapk"], sem="csts", barrier=True)
    P.dma("gpsimd", cdma(bq_c[:, :], bq_c_d), writes=["bq_c"], sem="csts", barrier=True)
    P.dma("gpsimd", cdma(bk_c[:, :], bk_c_d), writes=["bk_c"], sem="csts", barrier=True)
    P.dma("gpsimd", cdma(bv_r[:, :], bv_r_d), writes=["bv_r"], sem="csts", barrier=True)
    P.dma("sync", cdma(lnw_c[:, :], lnw_c_d), writes=["lnw_c"], sem="cstsy", barrier=True)
    P.dma("sync", cdma(lnb_c[:, :], lnb_c_d), writes=["lnb_c"], sem="cstsy", barrier=True)

    P.comp("vector", lambda e: e.memset(ones_col_b[:, :], 1.0), writes=["ones_col_b"])
    P.comp("vector", lambda e: e.memset(ones_col_f[:, :], 1.0), writes=["ones_col_f"])
    P.comp("vector", lambda e: e.memset(ones_row_f[:, :], 1.0), writes=["ones_row_f"])
    P.comp("vector", lambda e: e.memset(ones_row_bb[:, :], 1.0), writes=["ones_row_bb"])
    P.comp("vector", lambda e: e.memset(ones512_b[:, :], 1.0), writes=["ones512_b"])
    P.comp("vector", lambda e: e.memset(eps1_col[:, :], EPS1), writes=["eps1_col"])
    P.comp("vector", lambda e: e.memset(nhalf_col[:, :], -0.5), writes=["nhalf_col"])

    def prep_blk(w_d, mapw, mres, eff_d, tag, blk, cd_only=False, waits=()):
        sslot = blk % 2
        wres = f"stg{sslot}"
        ldop = P.dma("gpsimd",
              (lambda sslot, blk, w_d: lambda e: e.dma_start(
                  out=r3(stg[sslot], D)[:, 0:2, :],
                  in_=w_d.rearrange("(b p) k -> p b k", p=128)[:, 2 * blk:2 * blk + 2, :]))(
                  sslot, blk, w_d),
              writes=[wres], sem=f"stg{sslot}", waits=waits)
        for hh in range(2):
            h = 2 * blk + hh
            psx, pre = (ps_cd, "ps_cd") if cd_only else (
                (ps_ab, "ps_ab") if hh % 2 == 0 else (ps_cd, "ps_cd"))
            for kc in range(NK):
                P.comp("tensor",
                       (lambda sslot, hh, kc, psx: lambda e: e.matmul(
                           psx[:, kc * M:(kc + 1) * M],
                           r3(stg[sslot], D)[:, hh, kc * 128:(kc + 1) * 128],
                           mapw[:, :], start=True, stop=True))(sslot, hh, kc, psx),
                       reads=[wres, mres], writes=[f"{pre}{kc // 8}"])
            pslot = hh % 2
            P.comp("scalar",
                   (lambda psx, pslot: lambda e: e.activation(
                       prep_st[pslot][:, :], psx[:, :], AF.Copy))(psx, pslot),
                   reads=[f"{pre}0", f"{pre}1"], writes=[f"prep_st{pslot}"])
            P.dma("sync",
                  (lambda pslot, h, eff_d: lambda e: e.dma_start(
                      out=r3(eff_d.ap(), H * M)[:, :, h * M:(h + 1) * M],
                      in_=r3(prep_st[pslot], M)[:, :, :]))(pslot, h, eff_d),
                  reads=[f"prep_st{pslot}"], writes=[f"{tag}effd_h{h}"],
                  sem=f"effout{pslot}")
        return ldop

    def emit_prep_q_blk(blk, waits=()):
        return prep_blk(wq_d, wmapq, "wmapq", weq_d, "q", blk, waits=waits)

    def emit_prep_k_blk(blk, waits=()):
        return prep_blk(wk_d, wmapk, "wmapk", wek_d, "k", blk, cd_only=True, waits=waits)

    def emit_bias_rows():
        for (bc, bres, mapw, mres, brow, brres) in (
                (bq_c, "bq_c", wmapq, "wmapq", bq_row, "bq_row"),
                (bk_c, "bk_c", wmapk, "wmapk", bk_row, "bk_row")):
            for h in range(H):
                P.comp("tensor",
                       (lambda bc, mapw, h: lambda e: e.matmul(
                           ps_ef[0:1, h * M:(h + 1) * M], bc[:, h:h + 1], mapw[:, :],
                           start=True, stop=True))(bc, mapw, h),
                       reads=[bres, mres], writes=[f"ps_ef{h // 8}"])
            P.comp("scalar",
                   (lambda brow: lambda e: e.activation(brow[:, :], ps_ef[0:1, 0:1024],
                                                        AF.Copy))(brow),
                   reads=["ps_ef0", "ps_ef1"], writes=[brres])


    xn_fm3 = r3(xn_fm, T)
    xn_tm3 = r3(xn_tm, D)
    weq3d = r3(weq_d.ap(), H * M)

    def emit_effs_load(cch):
        slot = cch % 3
        P.dma("sync",
              (lambda slot, cch: lambda e: e.dma_start(
                  out=r3(effs[slot], 128)[:, :, :],
                  in_=weq3d[:, :, cch * 128:(cch + 1) * 128]))(slot, cch),
              writes=[f"effs{slot}"], reads=[f"qeffd_h{2 * cch}", f"qeffd_h{2 * cch + 1}"],
              sem=f"effs{slot}")

    for i in range(NT):
        slot = i % 3
        xr = f"xs{slot}"
        P.dma("gpsimd",
              (lambda slot, i: lambda e: e.dma_start(out=xs[slot][:, :],
                                                     in_=x_d[i * 128:(i + 1) * 128, :]))(slot, i),
              writes=[xr], sem=f"xs{slot}")
        emit_prep_q_blk(i)
        if i in (2, 3, 4):
            emit_effs_load(i - 2)
        j = i % 2
        for c in range(4):
            P.comp("vector",
                   (lambda slot, c, j: lambda e: e.bn_stats(st24[j][:, c * 6:(c + 1) * 6],
                                                            xs[slot][:, c * 512:(c + 1) * 512]))(slot, c, j),
                   reads=[xr], writes=[f"st24_{j}"])
        P.comp("vector", (lambda j: lambda e: e.bn_aggr(mv[j][:, :], st24[j][:, :]))(j),
               reads=[f"st24_{j}"], writes=[f"mv{j}"])
        P.comp("scalar", (lambda j: lambda e: e.activation(tca[j][:, :], mv[j][:, 1:2], AF.Ln,
                                                           bias=eps1_col[:, :]))(j),
               reads=[f"mv{j}", "eps1_col"], writes=[f"tca{j}"])
        P.comp("scalar", (lambda j: lambda e: e.activation(r1c[j][:, :], tca[j][:, :], AF.Exp,
                                                           scale=nhalf_col[:, :]))(j),
               reads=[f"tca{j}", "nhalf_col"], writes=[f"r1c{j}"])
        P.comp("vector", (lambda j: lambda e: e.tensor_mul(tcb[j][:, :], r1c[j][:, :], r1c[j][:, :]))(j),
               reads=[f"r1c{j}"], writes=[f"tcb{j}"])
        P.comp("vector", (lambda j: lambda e: e.tensor_mul(tcb[j][:, :], tcb[j][:, :], mv[j][:, 1:2]))(j),
               reads=[f"mv{j}"], writes=[f"tcb{j}"])
        P.comp("scalar", (lambda j: lambda e: e.activation(tcb[j][:, :], tcb[j][:, :], AF.Ln,
                                                           bias=eps1_col[:, :]))(j),
               reads=["eps1_col"], writes=[f"tcb{j}"])
        P.comp("scalar",
               (lambda i, j: lambda e: e.activation(r2cs[:, i:i + 1], tcb[j][:, :], AF.Exp,
                                                    scale=nhalf_col[:, :]))(i, j),
               reads=[f"tcb{j}", "nhalf_col"], writes=["r2cs"])
        P.comp("vector",
               (lambda slot, i, j: lambda e: e.tensor_scalar(xn_tm3[:, i, :], xs[slot][:, :],
                                                             mv[j][:, 0:1], r1c[j][:, :],
                                                             ALU.subtract, ALU.mult))(slot, i, j),
               reads=[xr, f"mv{j}", f"r1c{j}"], writes=[f"xnt{i}"])
        for half in range(2):
            pst, pr = (ps_t1, "ps_t1") if half == 0 else (ps_t2, "ps_t2")
            for kk in range(8):
                kc = half * 8 + kk
                P.comp("tensor",
                       (lambda pst, kk, kc, i: lambda e: e.transpose(
                           pst[:, kk * 128:(kk + 1) * 128],
                           xn_tm3[:, i, kc * 128:(kc + 1) * 128], ident[:, :]))(pst, kk, kc, i),
                       reads=[f"xnt{i}", "ident"], writes=[pr])
            P.comp("vector",
                   (lambda pst, half, i: lambda e: e.tensor_copy(
                       xn_fm3[:, half * 8:(half + 1) * 8, i * 128:(i + 1) * 128],
                       r3(pst, 128)[:, :, :]))(pst, half, i),
                   reads=[pr], writes=[f"xnf{i}"])

    XNF_ALL = [f"xnf{i}" for i in range(NT)]
    XNT_ALL = [f"xnt{i}" for i in range(NT)]
    emit_bias_rows()

    qp3 = r3(qp, T)
    kp_tm3 = r3(kp_tm, 1024)

    def elu_into(dst_ap_fn, dst_res, psx, pre):
        P.comp("vector",
               (lambda psx, dst_ap_fn: lambda e: e.tensor_scalar(dst_ap_fn(), psx[:, :], 0.0, None,
                                                                 ALU.max))(psx, dst_ap_fn),
               reads=[f"{pre}0", f"{pre}1"], writes=[dst_res])
        P.comp("vector",
               (lambda psx: lambda e: e.tensor_scalar(elu1[:, :], psx[:, :], 0.0, None,
                                                      ALU.min))(psx),
               reads=[f"{pre}0", f"{pre}1"], writes=["elu1"])
        P.comp("scalar", lambda e: e.activation(elu1[:, :], elu1[:, :], AF.Exp), writes=["elu1"])
        P.comp("vector",
               (lambda dst_ap_fn: lambda e: e.tensor_add(dst_ap_fn(), dst_ap_fn(), elu1[:, :]))(dst_ap_fn),
               reads=["elu1"], writes=[dst_res])

    for cch in range(8):
        emit_prep_k_blk(cch)
        slot = cch % 3
        eres = f"effs{slot}"
        psx, pre = (ps_ab, "ps_ab") if cch % 2 == 0 else (ps_ef, "ps_ef")
        for kc in range(NK):
            for half in range(2):
                P.comp("tensor",
                       (lambda slot, kc, half, psx: lambda e: e.matmul(
                           psx[:, half * 512:(half + 1) * 512],
                           effs[slot][:, kc * 128:(kc + 1) * 128],
                           xn_fm3[:, kc, half * 512:(half + 1) * 512],
                           start=(kc == 0), stop=False))(slot, kc, half, psx),
                       reads=[eres] + XNF_ALL[half * 4:(half + 1) * 4], writes=[f"{pre}{half}"])
        for half in range(2):
            P.comp("tensor",
                   (lambda cch, half, psx: lambda e: e.matmul(
                       psx[:, half * 512:(half + 1) * 512],
                       bq_row[:, cch * 128:(cch + 1) * 128],
                       ones512_b[:, :],
                       start=False, stop=True))(cch, half, psx),
                   reads=["bq_row", "ones512_b"], writes=[f"{pre}{half}"])
        qslot = cch % 2
        elu_into((lambda qslot=qslot: qp3[:, qslot, :]), f"qpc{qslot}", psx, pre)
        P.dma("sync",
              (lambda cch, qslot: lambda e: e.dma_start(
                  out=qp_d[:, cch * T:(cch + 1) * T], in_=qp3[:, qslot, :]))(cch, qslot),
              reads=[f"qpc{qslot}"], sem=f"outq{qslot}")
        if cch + 3 < 8:
            emit_effs_load(cch + 3)

    wfa3in = wfa_d.rearrange("(c p) o -> p c o", p=128)
    P.dma("gpsimd", lambda e: e.dma_start(out=wfa[:, :], in_=wfa3in[:, 0, :]),
          writes=["wfas"], sem="wfal")
    P.comp("vector",
           lambda e: e.reduce_max(am_col[:, :], wfa[:, :], axis=AXX,
                                  apply_absolute_value=True),
           reads=["wfas"], writes=["am_col"])
    wfb_out0 = P.dma("sync", lambda e: e.dma_start(out=wfb_d[:, 0:D], in_=wfa[:, :]),
                     reads=["wfas"], sem="outa")
    woa3in = woa_d.rearrange("(c p) o -> p c o", p=128)
    P.dma("gpsimd", lambda e: e.dma_start(out=woa[:, :], in_=woa3in[:, 0, :]),
          writes=["woas"], sem="woal")
    wob_out0 = P.dma("sync", lambda e: e.dma_start(out=wob_d[:, 0:D], in_=woa[:, :]),
                     reads=["woas"], sem="outa")

    wek3d = r3(wek_d.ap(), H * M)
    for j in range(2):
        P.dma("sync",
              (lambda j: lambda e: e.dma_start(
                  out=r3(stg[j], 1024)[:, :, :],
                  in_=wek3d[:, j * 8:(j + 1) * 8, :]))(j),
              writes=[f"stg{j}"], reads=[f"keffd_h{h}" for h in range(H)], sem=f"stg{j}")
    stgk = [r3(stg[0], 1024), r3(stg[1], 1024)]
    for i in range(NT):
        psx, pre = (ps_ab, "ps_ab") if i % 2 == 0 else (ps_ef, "ps_ef")
        for kc in range(NK):
            j, kk = (0, kc) if kc < 8 else (1, kc - 8)
            for half in range(2):
                P.comp("tensor",
                       (lambda i, kc, half, psx, j, kk: lambda e: e.matmul(
                           psx[:, half * 512:(half + 1) * 512],
                           xn_fm3[:, kc, i * 128:(i + 1) * 128],
                           stgk[j][:, kk, half * 512:(half + 1) * 512],
                           start=(kc == 0), stop=False))(i, kc, half, psx, j, kk),
                       reads=[f"xnf{i}", f"stg{j}"], writes=[f"{pre}{half}"])
        for half in range(2):
            P.comp("tensor",
                   (lambda half, psx: lambda e: e.matmul(
                       psx[:, half * 512:(half + 1) * 512],
                       ones_row_bb[0:1, 0:128],
                       bk_row[:, half * 512:(half + 1) * 512],
                       start=False, stop=True))(half, psx),
                   reads=["bk_row", "ones_row_bb"], writes=[f"{pre}{half}"])
        elu_into((lambda i=i: kp_tm3[:, i, :]), f"kpt{i}", psx, pre)
    KPT_ALL = [f"kpt{i}" for i in range(NT)]

    fillers = []
    _lists = {"r2": [], "xn2": [], "ksum": [], "conv": []}
    _cur = ["r2"]

    def F(fn):
        _lists[_cur[0]].append(fn)

    for i in range(0, NT, 2):
        def r2_pair(i=i):
            for ii in (i, i + 1):
                P.comp("vector",
                       (lambda ii: lambda e: e.tensor_scalar(diag[:, :], ident[:, :],
                                                             r2cs[:, ii:ii + 1], None,
                                                             ALU.mult))(ii),
                       reads=["ident", "r2cs"], writes=["diagr"])
                P.comp("tensor",
                       (lambda ii: lambda e: e.matmul(ps_cd[0:1, ii * 128:(ii + 1) * 128],
                                                      ones_col_f[:, :], diag[:, :],
                                                      start=True, stop=True))(ii),
                       reads=["diagr", "ones_col_f"], writes=[f"ps_cd{ii // 4}"])
        F(r2_pair)

    def r2_finish():
        P.comp("scalar", lambda e: e.activation(r2row[:, :], ps_cd[0:1, 0:1024], AF.Copy),
               reads=["ps_cd0", "ps_cd1"], writes=["r2row"])
        for half in range(2):
            P.comp("tensor",
                   (lambda half: lambda e: e.matmul(
                       ps_cd[:, half * 512:(half + 1) * 512], ones_row_f[:, :],
                       r2row[:, half * 512:(half + 1) * 512], start=True, stop=True))(half),
                   reads=["r2row", "ones_row_f"], writes=[f"ps_cd{half}"])
            P.comp("vector",
                   (lambda half: lambda e: e.tensor_copy(
                       B_r2[:, half * 512:(half + 1) * 512],
                       ps_cd[:, half * 512:(half + 1) * 512]))(half),
                   reads=[f"ps_cd{half}"], writes=["B_r2"])
    F(r2_finish)



    _cur[0] = "xn2"
    for kc in range(NK):
        def xn2_pair(kc=kc):
            P.comp("vector",
                   (lambda kc: lambda e: e.tensor_mul(xn_fm3[:, kc, :], xn_fm3[:, kc, :],
                                                      B_r2[:, :]))(kc),
                   reads=["B_r2"] + XNF_ALL, writes=[f"x2f{kc}"])
            P.comp("vector",
                   (lambda kc: lambda e: e.tensor_scalar(xn_fm3[:, kc, :], xn_fm3[:, kc, :],
                                                         lnw_c[:, kc:kc + 1], lnb_c[:, kc:kc + 1],
                                                         ALU.mult, ALU.add))(kc),
                   reads=["lnw_c", "lnb_c"], writes=[f"x2f{kc}"])
        F(xn2_pair)
        if kc == 7 or kc == 15:
            def xn2_out(j=kc // 8):
                xn2_outs.append(P.dma("sync",
                                      (lambda j: lambda e: e.dma_start(
                                          out=xn2_d[:, j * 8 * T:(j + 1) * 8 * T],
                                          in_=xn_fm[:, j * 8 * T:(j + 1) * 8 * T]))(j),
                                      reads=[f"x2f{k}" for k in range(j * 8, (j + 1) * 8)],
                                      sem="outx"))
            F(xn2_out)

    _cur[0] = "ksum"
    for h in range(H):
        def ksum_head(h=h):
            for i in range(NT):
                P.comp("tensor",
                       (lambda i, h: lambda e: e.matmul(
                           ps_cd[0:64, h:h + 1],
                           kp_tm3[:, i, h * M:(h + 1) * M], ones_col_b[:, :],
                           start=(i == 0), stop=(i == NT - 1)))(i, h),
                       reads=[f"kpt{i}", "ones_col_b"], writes=["ps_cd0"])
        F(ksum_head)

    def ksum_evac():
        P.comp("vector", lambda e: e.tensor_copy(ksum_cf[:, :], ps_cd[0:64, 0:16]),
               reads=["ps_cd0"], writes=["ksum_cf"])
        P.comp("vector", lambda e: e.memset(kv_sb[:, 1024:1040], 0.0), writes=["kv_sb"])
        P.comp("vector", lambda e: e.tensor_copy(kv_sb[0:64, 1024:1040], ksum_cf[:, :]),
               reads=["ksum_cf"], writes=["kv_sb"])
    F(ksum_evac)

    _cur[0] = "conv"
    for h in range(0, H, 2):
        def conv_pair(h=h):
            for hh in (h, h + 1):
                P.comp("vector",
                       (lambda hh: lambda e: e.tensor_scalar(diag[0:64, 0:64], ident[0:64, 0:64],
                                                             ksum_cf[:, hh:hh + 1], None,
                                                             ALU.mult))(hh),
                       reads=["ident", "ksum_cf"], writes=["diagr"])
                P.comp("tensor",
                       (lambda hh: lambda e: e.matmul(ps_cd[0:1, hh * M:(hh + 1) * M],
                                                      ones_col_f[0:64, :], diag[0:64, 0:64],
                                                      start=True, stop=True))(hh),
                       reads=["diagr", "ones_col_f"], writes=[f"ps_cd{hh // 8}"])
        F(conv_pair)

    def ksum_row_evac():
        P.comp("scalar", lambda e: e.activation(ksum_rb[:, :], ps_cd[0:1, 0:1024], AF.Copy),
               reads=["ps_cd0", "ps_cd1"], writes=["ksum_rb"])
    F(ksum_row_evac)


    fillers.extend(_lists["r2"])
    xl = _lists["xn2"]
    kl = _lists["ksum"] + _lists["conv"]
    xi = ki = 0
    while xi < len(xl) or ki < len(kl):
        if xi < len(xl):
            fillers.append(xl[xi]); xi += 1
            if xi < len(xl) and xi in (8, 17):
                fillers.append(xl[xi]); xi += 1
        for _ in range(2):
            if ki < len(kl):
                fillers.append(kl[ki]); ki += 1

    xn2_outs = []
    xn_tmD = r3(xn_tm, D)
    def emit_wv_load(kc):
        slot = kc % 5
        P.dma("gpsimd",
              (lambda slot, kc: lambda e: e.dma_start(out=whs[slot][:, :],
                                                      in_=wvT_d[kc * 128:(kc + 1) * 128, :]))(slot, kc),
              writes=[f"whs{slot}"], sem=f"whs{slot}")

    for kc in range(5):
        emit_wv_load(kc)
    P.dma("gpsimd", lambda e: e.dma_start(out=wfa[:, :], in_=wfa3in[:, 1, :]),
          reads=["wfas"], writes=["wfas"], sem="wfal")
    P.comp("vector",
           lambda e: e.reduce_max(am_tmp[:, :], wfa[:, :], axis=AXX,
                                  apply_absolute_value=True),
           reads=["wfas"], writes=["am_tmp"])
    P.comp("vector", lambda e: e.tensor_max(am_col[:, :], am_col[:, :], am_tmp[:, :]),
           reads=["am_tmp"], writes=["am_col"])
    wfb_out1 = P.dma("sync", lambda e: e.dma_start(out=wfb_d[:, D:2 * D], in_=wfa[:, :]),
                     reads=["wfas"], sem="outa")
    P.dma("gpsimd", lambda e: e.dma_start(out=woa[:, :], in_=woa3in[:, 1, :]),
          reads=["woas"], writes=["woas"], sem="woal")
    for kc in range(NK):
        for _ in range(3):
            if fillers:
                fillers.pop(0)()
        slot = kc % 5
        wres = f"whs{slot}"
        psx, pre = (ps_ab, "ps_ab") if kc % 2 == 0 else (ps_ef, "ps_ef")
        for half in range(2):
            for i in range(NT):
                P.comp("tensor",
                       (lambda kc, half, i, psx: lambda e: e.matmul(
                           psx[:, half * 512:(half + 1) * 512],
                           xn_tmD[:, i, kc * 128:(kc + 1) * 128],
                           kp_tm3[:, i, half * 512:(half + 1) * 512],
                           start=(i == 0), stop=(i == NT - 1)))(kc, half, i, psx),
                       reads=[f"xnt{i}", f"kpt{i}"], writes=[f"{pre}{half}"])
        aslot = kc % 2
        P.comp("vector",
               (lambda psx, aslot: lambda e: e.tensor_copy(Ats[aslot][:, :], psx[:, :]))(psx, aslot),
               reads=[f"{pre}0", f"{pre}1"], writes=[f"Ats{aslot}"])
        kvp, kvr = (ps_ef, "ps_ef") if kc % 2 == 0 else (ps_ab, "ps_ab")
        for h in range(H):
            r0 = 0 if h < 8 else 64
            P.comp("tensor",
                   (lambda h, aslot, slot, kvp, r0: lambda e: e.matmul(
                       kvp[r0:r0 + 64, (h % 8) * 128:(h % 8) * 128 + 128],
                       Ats[aslot][:, h * M:(h + 1) * M],
                       whs[slot][:, h * 128:(h + 1) * 128],
                       start=True, stop=True))(h, aslot, slot, kvp, r0),
                   reads=[f"Ats{aslot}", wres], writes=[f"{kvr}{(h % 8) // 4}"])
        if kc == 0:
            P.comp("vector",
                   (lambda kvp: lambda e: e.tensor_copy(kv_sb[:, 0:1024], kvp[:, :]))(kvp),
                   reads=[f"{kvr}0", f"{kvr}1"], writes=["kv_sb"])
        else:
            P.comp("vector",
                   (lambda kvp: lambda e: e.tensor_add(kv_sb[:, 0:1024], kv_sb[:, 0:1024],
                                                       kvp[:, :]))(kvp),
                   reads=[f"{kvr}0", f"{kvr}1"], writes=["kv_sb"])
        if kc + 5 < NK:
            emit_wv_load(kc + 5)
    while fillers:
        fillers.pop(0)()

    for h in range(H):
        r0 = 0 if h < 8 else 64
        P.comp("tensor",
               (lambda h, r0: lambda e: e.matmul(
                   ps_cd[r0:r0 + 64, (h % 8) * 128:(h % 8) * 128 + 128],
                   ksum_rb[:, h * M:(h + 1) * M],
                   bv_r[:, h * 128:(h + 1) * 128],
                   start=True, stop=True))(h, r0),
               reads=["ksum_rb", "bv_r"], writes=[f"ps_cd{(h % 8) // 4}"])
    P.comp("vector",
           lambda e: e.tensor_add(kv_sb[:, 0:1024], kv_sb[:, 0:1024], ps_cd[:, :]),
           reads=["ps_cd0", "ps_cd1"], writes=["kv_sb"])
    t_kv_out = P.dma("gpsimd", lambda e: e.dma_start(out=kvm_d, in_=kv_sb[:, 0:1024]),
                     reads=["kv_sb"], sem="outk")
    t_ks_out = P.dma("sync", lambda e: e.dma_start(out=kvs_d, in_=kv_sb[:, 1024:1040]),
                     reads=["kv_sb"], sem="outk")

    while fillers:
        fillers.pop(0)()

    P.comp("scalar", lambda e: e.activation(am_cb[:, :], am_col[:, :], AF.Copy),
           reads=["am_col"], writes=["am_cb"])
    P.comp("tensor", lambda e: e.matmul(ps_cd[0:1, 0:128], am_cb[:, :], ident[:, :],
                                        start=True, stop=True),
           reads=["am_cb", "ident"], writes=["ps_cd0"])
    P.comp("vector", lambda e: e.reduce_max(am1[:, :], ps_cd[0:1, 0:128], axis=AXX),
           reads=["ps_cd0"], writes=["am1"])
    amax_out = P.dma("sync", lambda e: e.dma_start(out=amax_d, in_=am1[:, :]),
                     reads=["am1"], sem="outa")
    wob_out = P.dma("sync", lambda e: e.dma_start(out=wob_d[:, D:2 * D], in_=woa[:, :]),
                    reads=["woas"], sem="outa")

    dbg_outs = []
    if DEBUG:
        dbg_outs.append(P.dma("sync", lambda e: e.dma_start(out=kptm_dbg, in_=kp_tm[:, :]),
                              reads=KPT_ALL, sem="outdbg"))
        dbg_outs.append(P.dma("sync", lambda e: e.dma_start(out=stats_dbg, in_=r2cs[:, :]),
                              reads=["r2cs"], sem="outdbg"))
    qp_outs = [op for op in P.ops if op.dma and op.stream and op.stream.startswith('outq')]
    P.comp("sync", lambda e: e.nop(),
           waits=[t_kv_out, t_ks_out, amax_out, wfb_out0, wfb_out1, wob_out0, wob_out] + qp_outs + xn2_outs + dbg_outs)

    P.emit(ctx)
    return nc



def build_launch2():
    nc = bass.Bass("TRN2", target_bir_lowering=False, detect_race_conditions=False)
    ctx = ExitStack()

    kvallm_d = nc.dram_tensor("kvallm", [NCORES * 128, 1024], BF16, kind="ExternalInput").ap()
    kvalls_d = nc.dram_tensor("kvalls", [NCORES * 128, 16], F32, kind="ExternalInput").ap()
    qp_d = nc.dram_tensor("qp", [128, 8 * T], BF16, kind="ExternalInput").ap()
    xn2_d = nc.dram_tensor("xn2", [128, NK * T], BF16, kind="ExternalInput").ap()
    x_d = nc.dram_tensor("x", [T, D], F32, kind="ExternalInput").ap()
    wfb_d = nc.dram_tensor("wfb", [128, NK * D], BF16, kind="ExternalInput").ap()
    wob_d = nc.dram_tensor("wob", [128, NK * D], BF16, kind="ExternalInput").ap()
    bo_r_d = nc.dram_tensor("bo_r", [1, D], F32, kind="ExternalInput").ap()
    gamma_r_d = nc.dram_tensor("gamma_r", [1, D], F32, kind="ExternalInput").ap()
    beta_r_d = nc.dram_tensor("beta_r", [1, D], F32, kind="ExternalInput").ap()
    amax8_d = nc.dram_tensor("amax8", [1, NCORES], F32, kind="ExternalInput").ap()
    out_d = nc.dram_tensor("out", [T, D], F32, kind="ExternalOutput").ap()

    A = nc.alloc_sbuf_tensor
    ones_row_f = A("ones_row_f", [1, 128], F32).ap()
    ones_row_b2 = A("ones_row_b2", [1, 128], BF16).ap()
    nEPS = A("nEPS", [1, 1], F32).ap()
    nm1 = A("nm1", [1, 1], F32).ap()
    kv_acc = A("kv_acc", [128, 1040], F32).ap()
    kvbig = A("kvbig", [128, 4 * 1024], BF16).ap()
    ksbig = A("ksbig", [128, NCORES * 16], F32).ap()
    kvb = A("kvb", [128, 2048], BF16).ap()
    ksum_cb = A("ksum_cb", [128, H], BF16).ap()
    qp = A("qps", [128, 8 * T], BF16).ap()
    xn2 = A("xn2s", [128, NK * T], BF16).ap()
    att = A("atts", [128, NK * T], BF16).ap()
    gob = [A(f"gob{j}", [128, NT * 512], BF16).ap() for j in range(2)]
    wsg = [A(f"wsg{j}", [128, NK * 512], BF16).ap() for j in range(2)]
    B_gs = A("B_gs", [128, D], BF16).ap()
    B_beta = A("B_beta", [128, D], BF16).ap()
    B_bo = A("B_bo", [128, D], BF16).ap()
    fsc = [A(f"fsc{j}", [128, 512], F32).ap() for j in range(2)]
    gamma_r = A("gamma_rs", [1, D], BF16).ap()
    beta_r = A("beta_rs", [1, D], BF16).ap()
    bo_r = A("bo_rs", [1, D], BF16).ap()
    am8 = A("am8", [1, NCORES], F32).ap()
    am_sc = A("am_sc", [1, 1], F32).ap()
    scale_col = A("scale_col", [128, 1], F32).ap()
    lnr = [A(f"lnr{j}", [1, T], F32).ap() for j in range(2)]
    lnrb = [A(f"lnrb{j}", [1, T], BF16).ap() for j in range(2)]
    rbs = [A(f"rbs{j}", [128, T], F32).ap() for j in range(1)]
    xsb = [A(f"xsb{j}", [128, 512], F32).ap() for j in range(2)]
    osc = [A(f"osc{j}", [128, 512], F32).ap() for j in range(2)]

    APs = nc.alloc_psum_tensor
    ps_a = APs("ps_a", [128, 1024], F32).ap()
    ps_b = APs("ps_b", [128, 1024], F32).ap()
    ps_s = APs("ps_s", [128, 1024], F32).ap()
    ps_n = APs("ps_n", [1, 1024], F32).ap()

    P = Prog(nc)

    wob3 = r3(wob_d, D)
    wfb3 = r3(wfb_d, D)
    qp3 = r3(qp, T)
    xn23 = r3(xn2, T)
    att3 = r3(att, T)

    P.comp("vector", lambda e: e.memset(ones_row_f[:, :], 1.0), writes=["ones_row_f"])
    P.comp("vector", lambda e: e.memset(ones_row_b2[:, :], 1.0), writes=["ones_row_b2"])
    P.comp("vector", lambda e: e.memset(nEPS[:, :], EPS2), writes=["nEPS"])
    P.comp("vector", lambda e: e.memset(nm1[:, :], -1.0), writes=["nm1"])

    P.dma("gpsimd", lambda e: e.dma_start(out=bo_r[:, :], in_=bo_r_d), writes=["bo_r"], sem="csts", barrier=True)
    P.dma("gpsimd", lambda e: e.dma_start(out=gamma_r[:, :], in_=gamma_r_d), writes=["gamma_r"], sem="csts", barrier=True)
    P.dma("gpsimd", lambda e: e.dma_start(out=beta_r[:, :], in_=beta_r_d), writes=["beta_r"], sem="csts", barrier=True)
    t_wf0 = P.dma("gpsimd", lambda e: e.dma_start(out=r3(wsg[0], 512)[:, :, :],
                                                  in_=wfb3[:, :, 0:512]),
                  writes=["wsg0"], sem="wsg0")
    t_xn2 = P.dma("scalar", lambda e: e.dma_start(out=xn2[:, :], in_=xn2_d), writes=["xn2l"], sem="xnl")
    P.dma("scalar", lambda e: e.dma_start(out=qp[:, :], in_=qp_d), writes=["qp"], sem="qpl",
          waits=[t_wf0])

    P.dma("sync", lambda e: e.dma_start(out=am8[:, :], in_=amax8_d), writes=["am8"], sem="aml")
    P.comp("vector", lambda e: e.reduce_max(am_sc[:, :], am8[:, :], axis=AXX),
           reads=["am8"], writes=["am_sc"])
    P.comp("tensor", lambda e: e.matmul(ps_b[0:128, 0:1], ones_row_f[:, :], am_sc[:, :],
                                        start=True, stop=True),
           reads=["am_sc", "ones_row_f"], writes=["ps_b0"])
    P.comp("vector", lambda e: e.tensor_copy(scale_col[:, :], ps_b[0:128, 0:1]),
           reads=["ps_b0"], writes=["scale_col"])

    for q in range(4):
        bank = q % 2
        P.comp("tensor",
               (lambda q, bank: lambda e: e.matmul(
                   ps_b[:, bank * 512:(bank + 1) * 512], ones_row_b2[:, :],
                   bo_r[:, q * 512:(q + 1) * 512], start=True, stop=True))(q, bank),
               reads=["bo_r", "ones_row_b2"], writes=[f"ps_b{bank}"])
        P.comp("vector",
               (lambda q, bank: lambda e: e.tensor_copy(
                   B_bo[:, q * 512:(q + 1) * 512],
                   ps_b[:, bank * 512:(bank + 1) * 512]))(q, bank),
               reads=[f"ps_b{bank}"], writes=["B_bo"])
    for q in range(4):
        bank = q % 2
        P.comp("tensor",
               (lambda q, bank: lambda e: e.matmul(
                   ps_b[:, bank * 512:(bank + 1) * 512], ones_row_b2[:, :],
                   gamma_r[:, q * 512:(q + 1) * 512], start=True, stop=True))(q, bank),
               reads=["gamma_r", "ones_row_b2"], writes=[f"ps_b{bank}"])
        P.comp("vector",
               (lambda q, bank: lambda e: e.tensor_scalar(
                   B_gs[:, q * 512:(q + 1) * 512],
                   ps_b[:, bank * 512:(bank + 1) * 512],
                   scale_col[:, :], None, ALU.mult))(q, bank),
               reads=[f"ps_b{bank}", "scale_col"], writes=["B_gs"])
    for q in range(4):
        bank = q % 2
        P.comp("tensor",
               (lambda q, bank: lambda e: e.matmul(
                   ps_b[:, bank * 512:(bank + 1) * 512], ones_row_b2[:, :],
                   beta_r[:, q * 512:(q + 1) * 512], start=True, stop=True))(q, bank),
               reads=["beta_r", "ones_row_b2"], writes=[f"ps_b{bank}"])
        P.comp("vector",
               (lambda q, bank: lambda e: e.tensor_copy(
                   B_beta[:, q * 512:(q + 1) * 512],
                   ps_b[:, bank * 512:(bank + 1) * 512]))(q, bank),
               reads=[f"ps_b{bank}"], writes=["B_beta"])

    kvm3 = kvallm_d.rearrange("(c p) o -> p c o", p=128)
    kvb3 = r3(kvbig, 1024)
    t_kvm1 = P.dma("sync", lambda e: e.dma_start(out=kvb3[:, :, :], in_=kvm3[:, 0:4, :]),
                   writes=["kvbig"], sem="kva", waits=[t_wf0, t_xn2])
    P.dma("sync", lambda e: e.dma_start(out=r3(ksbig, 16)[:, :, :],
                                        in_=kvalls_d.rearrange("(c p) o -> p c o", p=128)),
          writes=["ksbig"], sem="ksl")
    P.comp("gpsimd", lambda e: e.tensor_copy(kv_acc[:, 0:1024], kvb3[:, 0, :]),
           reads=["kvbig"], writes=["kv_acc"])
    for c in range(1, 4):
        P.comp("gpsimd",
               (lambda c: lambda e: e.tensor_add(kv_acc[:, 0:1024], kv_acc[:, 0:1024],
                                                 kvb3[:, c, :]))(c),
               reads=["kvbig"], writes=["kv_acc"])
    P.dma("sync", lambda e: e.dma_start(out=kvb3[:, :, :], in_=kvm3[:, 4:8, :]),
          writes=["kvbig"], sem="kva")
    for c in range(4):
        P.comp("gpsimd",
               (lambda c: lambda e: e.tensor_add(kv_acc[:, 0:1024], kv_acc[:, 0:1024],
                                                 kvb3[:, c, :]))(c),
               reads=["kvbig"], writes=["kv_acc"])
    P.comp("gpsimd", lambda e: e.tensor_copy(kv_acc[:, 1024:1040], r3(ksbig, 16)[:, 0, :]),
           reads=["ksbig"], writes=["kv_acc"])
    for c in range(1, NCORES):
        P.comp("gpsimd",
               (lambda c: lambda e: e.tensor_add(kv_acc[:, 1024:1040], kv_acc[:, 1024:1040],
                                                 r3(ksbig, 16)[:, c, :]))(c),
               reads=["ksbig"], writes=["kv_acc"])
    P.comp("gpsimd", lambda e: e.tensor_copy(ksum_cb[0:64, :], kv_acc[0:64, 1024:1040]),
           reads=["kv_acc"], writes=["ksum_cb"])
    P.dma("sync", lambda e: e.dma_start(out=ksum_cb[64:128, :], in_=ksum_cb[0:64, :]),
          reads=["ksum_cb"], writes=["ksum_cb"], sem="ksd")
    ka3 = r3(kv_acc[:, 0:1024], 128)
    kv3 = r3(kvb, 128)
    P.dma("gpsimd", lambda e: e.dma_start(out=kv3[0:64, 0:8:2, :], in_=ka3[0:64, 0:8:2, :]),
          reads=["kv_acc"], writes=["kvb"], sem="dup")
    P.dma("gpsimd", lambda e: e.dma_start(out=kv3[64:128, 1:8:2, :], in_=ka3[0:64, 1:8:2, :]),
          reads=["kv_acc"], writes=["kvb"], sem="dup")
    P.dma("gpsimd", lambda e: e.dma_start(out=kv3[0:64, 8:16:2, :], in_=ka3[64:128, 0:8:2, :]),
          reads=["kv_acc"], writes=["kvb"], sem="dup")
    P.dma("gpsimd", lambda e: e.dma_start(out=kv3[64:128, 9:16:2, :], in_=ka3[64:128, 1:8:2, :]),
          reads=["kv_acc"], writes=["kvb"], sem="dup")
    P.dma("gpsimd", lambda e: e.dma_start(out=r3(wsg[1], 512)[:, :, :],
                                          in_=wob3[:, :, 0:512]),
          writes=["wsg1"], sem="wsg1")

    def emit_ffn_block(ob):
        if ob > 0:
            P.dma("gpsimd",
                  (lambda ob: lambda e: e.dma_start(
                      out=r3(wsg[0], 512)[:, :, :],
                      in_=wfb3[:, :, ob * 512:(ob + 1) * 512]))(ob),
                  writes=["wsg0"], sem="wsg0")
        P.comp("scalar", lambda e: e.activation(wsg[0][:, :], wsg[0][:, :], AF.Sign),
               writes=["wsg0"])
        gslot = ob % 2
        for i in range(NT):
            bank = i % 2
            pv = f"ps_s{bank}"
            for kc in range(NK):
                P.comp("tensor",
                       (lambda i, kc, bank: lambda e: e.matmul(
                           ps_s[:, bank * 512:(bank + 1) * 512],
                           xn23[:, kc, i * 128:(i + 1) * 128],
                           r3(wsg[0], 512)[:, kc, :],
                           start=(kc == 0), stop=(kc == NK - 1)))(i, kc, bank),
                       reads=["xn2l", "wsg0"], writes=[pv])
            P.comp("vector",
                   (lambda i, ob, bank: lambda e: e.tensor_mul(
                       fsc[bank][:, :], ps_s[:, bank * 512:(bank + 1) * 512],
                       B_gs[:, ob * 512:(ob + 1) * 512]))(i, ob, bank),
                   reads=[pv, "B_gs"], writes=[f"fsc{bank}"])
            P.comp("vector",
                   (lambda ob, bank: lambda e: e.tensor_add(
                       fsc[bank][:, :], fsc[bank][:, :],
                       B_beta[:, ob * 512:(ob + 1) * 512]))(ob, bank),
                   reads=["B_beta"], writes=[f"fsc{bank}"])
            P.comp("scalar",
                   (lambda i, bank, gslot: lambda e: e.activation(
                       gob[gslot][:, i * 512:(i + 1) * 512], fsc[bank][:, :],
                       AF.Silu))(i, bank, gslot),
                   reads=[f"fsc{bank}"], writes=[f"gob{gslot}"])

    def emit_attention():
        def emit_norm(h):
            p0 = (h % 2) * 64
            ch = h // 2
            hs = h % 2
            for half in range(2):
                P.comp("tensor",
                       (lambda h, half, p0, ch: lambda e: e.matmul(
                           ps_n[0:1, half * 512:(half + 1) * 512],
                           ksum_cb[p0:p0 + 64, h:h + 1],
                           qp3[p0:p0 + 64, ch, half * 512:(half + 1) * 512],
                           start=True, stop=True))(h, half, p0, ch),
                       reads=["ksum_cb", "qp"], writes=[f"ps_n{half}"])
            P.comp("scalar",
                   (lambda hs: lambda e: e.activation(
                       lnr[hs][:, :], ps_n[0:1, :], AF.Ln,
                       bias=nEPS[:, :]))(hs),
                   reads=["ps_n0", "ps_n1", "nEPS"], writes=[f"lnr{hs}"])
            P.comp("scalar",
                   (lambda hs: lambda e: e.activation(lnrb[hs][:, :], lnr[hs][:, :], AF.Exp,
                                                      scale=nm1[:, :]))(hs),
                   reads=["nm1", f"lnr{hs}"], writes=[f"lnrb{hs}"])

        def emit_apply(h):
            p0 = (h % 2) * 64
            ch = h // 2
            hs = h % 2
            for half in range(2):
                P.comp("tensor",
                       (lambda half, hs: lambda e: e.matmul(
                           ps_b[:, half * 512:(half + 1) * 512], ones_row_b2[:, :],
                           lnrb[hs][:, half * 512:(half + 1) * 512],
                           start=True, stop=True))(half, hs),
                       reads=[f"lnrb{hs}", "ones_row_b2"], writes=[f"ps_b{half}"])
            P.comp("vector",
                   lambda e: e.tensor_copy(rbs[0][:, :], ps_b[:, :]),
                   reads=["ps_b0", "ps_b1"], writes=["rbs0"])
            for half in range(2):
                P.comp("tensor",
                       (lambda h, half, p0, ch: lambda e: e.matmul(
                           ps_a[:, half * 512:(half + 1) * 512],
                           kvb[p0:p0 + 64, h * 128:(h + 1) * 128],
                           qp3[p0:p0 + 64, ch, half * 512:(half + 1) * 512],
                           start=True, stop=True))(h, half, p0, ch),
                       reads=["kvb", "qp"], writes=[f"ps_a{half}"])
            P.comp("vector",
                   (lambda h: lambda e: e.tensor_mul(
                       att3[:, h, :], ps_a[:, :], rbs[0][:, :]))(h),
                   reads=["ps_a0", "ps_a1", "rbs0"], writes=["att"])

        for h in range(H + 1):
            if h < H:
                emit_norm(h)
            if h >= 1:
                emit_apply(h - 1)

    out_dmas = []

    def emit_oproj_block(ob):
        if ob > 0:
            P.dma("gpsimd",
                  (lambda ob: lambda e: e.dma_start(
                      out=r3(wsg[1], 512)[:, :, :],
                      in_=wob3[:, :, ob * 512:(ob + 1) * 512]))(ob),
                  writes=["wsg1"], sem="wsg1")
        gslot = ob % 2
        for i in range(NT):
            bank = i % 2
            pv = f"ps_s{bank}"
            xslot = i % 2
            P.dma("sync",
                  (lambda i, ob, xslot: lambda e: e.dma_start(
                      out=xsb[xslot][:, :],
                      in_=x_d[i * 128:(i + 1) * 128, ob * 512:(ob + 1) * 512]))(i, ob, xslot),
                  writes=[f"xsb{xslot}"], sem=f"xsb{xslot}")
            for kc in range(NK):
                P.comp("tensor",
                       (lambda i, ob, kc, bank: lambda e: e.matmul(
                           ps_s[:, bank * 512:(bank + 1) * 512],
                           att3[:, kc, i * 128:(i + 1) * 128],
                           r3(wsg[1], 512)[:, kc, :],
                           start=(kc == 0), stop=(kc == NK - 1)))(i, ob, kc, bank),
                       reads=["att", "wsg1"], writes=[pv])
            P.comp("vector",
                   (lambda ob, bank: lambda e: e.tensor_add(
                       osc[bank][:, :], ps_s[:, bank * 512:(bank + 1) * 512],
                       B_bo[:, ob * 512:(ob + 1) * 512]))(ob, bank),
                   reads=[pv, "B_bo"], writes=[f"osc{bank}"])
            P.comp("vector",
                   (lambda i, bank, gslot: lambda e: e.tensor_mul(
                       osc[bank][:, :], osc[bank][:, :],
                       gob[gslot][:, i * 512:(i + 1) * 512]))(i, bank, gslot),
                   reads=[f"gob{gslot}"], writes=[f"osc{bank}"])
            P.comp("vector",
                   (lambda bank, xslot: lambda e: e.tensor_add(
                       osc[bank][:, :], osc[bank][:, :],
                       xsb[xslot][:, :]))(bank, xslot),
                   reads=[f"xsb{xslot}"], writes=[f"osc{bank}"])
            out_dmas.append(P.dma("sync",
                                  (lambda i, ob, bank: lambda e: e.dma_start(
                                      out=out_d[i * 128:(i + 1) * 128, ob * 512:(ob + 1) * 512],
                                      in_=osc[bank][:, :]))(i, ob, bank),
                                  reads=[f"osc{bank}"], sem=f"outo{bank}"))

    emit_ffn_block(0)
    emit_attention()
    emit_oproj_block(0)
    for ob in range(1, 4):
        emit_ffn_block(ob)
        emit_oproj_block(ob)

    P.comp("sync", lambda e: e.nop(), waits=out_dmas)

    P.emit(ctx)
    return nc



_NC1 = None
_NC2 = None


def kernel(**inputs):
    global _NC1, _NC2
    x_full = np.asarray(inputs["L0_attn_hidden_states"])
    wq = np.asarray(inputs["L0_attn_wq"])
    wk = np.asarray(inputs["L0_attn_wk"])
    wv = np.asarray(inputs["L0_attn_wv"])
    wo = np.asarray(inputs["L0_attn_wo"])
    bq = np.asarray(inputs["L0_attn_bq"])
    bk = np.asarray(inputs["L0_attn_bk"])
    bv = np.asarray(inputs["L0_attn_bv"])
    bo = np.asarray(inputs["L0_attn_bo"])
    wmq = np.asarray(inputs["L0_attn_wmap_q"])
    wmk = np.asarray(inputs["L0_attn_wmap_k"])
    wf = np.asarray(inputs["L0_ffn_weight"])
    gamma = np.asarray(inputs["L0_ffn_gamma"])
    beta = np.asarray(inputs["L0_ffn_beta"])
    lnw = np.asarray(inputs["L0_ffn_ln_weight"])
    lnb = np.asarray(inputs["L0_ffn_ln_bias"])

    C = np.ascontiguousarray
    wvT = C(wv.T)
    woT = C(wo.T)
    wfT = C(wf.T)
    wmapqT = C(wmq.T)
    wmapkT = C(wmk.T)
    bq_c = C(bq.reshape(H, DH).T)
    bk_c = C(bk.reshape(H, DH).T)
    lnw_c = C(lnw.reshape(NK, 128).T)
    lnb_c = C(lnb.reshape(NK, 128).T)
    bv_r = C(bv.reshape(1, D))
    bo_r = C(bo.reshape(1, D))
    gamma_r = C(gamma.reshape(1, D))
    beta_r = C(beta.reshape(1, D))
    ident = np.eye(128, dtype=BF)

    xt = x_full.reshape(8192, D)

    if _NC1 is None:
        _NC1 = build_launch1()
    in_maps1 = []
    for c in range(NCORES):
        in_maps1.append({
            "x": C(xt[c * T:(c + 1) * T, :]),
            "wq": wq, "wk": wk, "wvT": wvT,
            "wfa": C(wfT[c * 256:(c + 1) * 256, :]),
            "woa": C(woT[c * 256:(c + 1) * 256, :]),
            "wmapqT": wmapqT, "wmapkT": wmapkT,
            "bq_c": bq_c, "bk_c": bk_c, "bv_r": bv_r,
            "lnw_c": lnw_c, "lnb_c": lnb_c,
            "ident": ident,
        })
    res1 = run_bass_kernel_spmd(_NC1, in_maps1, core_ids=list(range(NCORES)))

    kvallm = np.concatenate([res1.results[c]["kvm"] for c in range(NCORES)], axis=0)
    kvalls = np.concatenate([res1.results[c]["kvs"] for c in range(NCORES)], axis=0)
    amax8 = np.concatenate([res1.results[c]["amax"] for c in range(NCORES)], axis=1)
    wfb = np.concatenate([res1.results[c]["wfb"] for c in range(NCORES)], axis=1)
    wob = np.concatenate([res1.results[c]["wob"] for c in range(NCORES)], axis=1)

    if _NC2 is None:
        _NC2 = build_launch2()
    in_maps2 = []
    for c in range(NCORES):
        in_maps2.append({
            "kvallm": kvallm,
            "kvalls": kvalls,
            "qp": res1.results[c]["qp"],
            "xn2": res1.results[c]["xn2"],
            "amax8": amax8,
            "x": C(xt[c * T:(c + 1) * T, :]),
            "wfb": wfb, "wob": wob, "bo_r": bo_r,
            "gamma_r": gamma_r, "beta_r": beta_r,
        })
    res2 = run_bass_kernel_spmd(_NC2, in_maps2, core_ids=list(range(NCORES)))

    out = np.concatenate([res2.results[c]["out"] for c in range(NCORES)], axis=0)
    return out.reshape(1, 8192, D)


# version 36 (final)
# speedup vs baseline: 1.0702x; 1.0382x over previous
import numpy as np
import ml_dtypes
from contextlib import ExitStack

import concourse.bass as bass
import concourse.mybir as mybir
from concourse.bass_utils import run_bass_kernel_spmd

F32 = mybir.dt.float32
BF16 = mybir.dt.bfloat16
AF = mybir.ActivationFunctionType
ALU = mybir.AluOpType
AXX = mybir.AxisListType.X

NCORES = 8
T = 1024
NT = 8
D = 2048
NK = 16
H = 16
DH = 128
M = 64
EPS1 = 1e-5
EPS2 = 1e-6
BF = ml_dtypes.bfloat16

ENG_SEM = {"tensor": "pe", "vector": "dve", "scalar": "act", "gpsimd": "pool", "sync": "spc"}
PER_DMA_SEMS = False
STREAM_EXCLUDE = set()


class Op:
    __slots__ = ("eng", "fn", "dma", "deps", "sem", "val", "marked", "stream", "idx")

    def __init__(self, eng, fn, dma, deps, stream=None, idx=0):
        self.eng, self.fn, self.dma, self.deps = eng, fn, dma, deps
        self.sem = None
        self.val = None
        self.marked = False
        self.stream = stream
        self.idx = idx


class Prog:

    def __init__(self, nc):
        self.nc = nc
        self.ops = []
        self.res = {}
        self.streams = {}
        self.ndma = 0

    def _edges(self, reads, writes, waits):
        deps = set()
        for o in waits:
            if o is not None:
                deps.add(o)
        for rn in reads:
            r = self.res.setdefault(rn, {"w": None, "r": []})
            if r["w"] is not None:
                deps.add(r["w"])
        for wn in writes:
            r = self.res.setdefault(wn, {"w": None, "r": []})
            if r["w"] is not None:
                deps.add(r["w"])
            for o in r["r"]:
                deps.add(o)
        return deps

    def _add(self, eng, fn, dma, reads, writes, waits, stream=None):
        op = Op(eng, fn, dma, self._edges(reads, writes, waits), stream, len(self.ops))
        self.ops.append(op)
        for rn in reads:
            self.res[rn]["r"].append(op)
        for wn in writes:
            self.res[wn]["w"] = op
            self.res[wn]["r"] = []
        return op

    def comp(self, eng, fn, reads=(), writes=(), waits=()):
        return self._add(eng, fn, False, reads, writes, waits)

    def dma(self, queue, fn, reads=(), writes=(), waits=(), sem=None, barrier=False):
        if sem is None:
            sem = f"u{self.ndma}"
            self.ndma += 1
        st = self.streams.setdefault(sem, {"count": 0, "barrier": barrier})
        st["count"] += 16
        op = self._add(queue, fn, True, reads, writes, waits, stream=sem)
        op.val = st["count"]
        return op

    def emit(self, ctx):
        nc = self.nc
        def skip_edge(dep, op):
            return (not dep.dma) and (not op.dma) and dep.eng == op.eng == "tensor"

        for op in self.ops:
            best = {}
            for dep in op.deps:
                if skip_edge(dep, op):
                    continue
                key = ("s", dep.stream) if dep.dma else ("e", dep.eng)
                cur = best.get(key)
                if cur is None or dep.idx > cur.idx:
                    best[key] = dep
            op.deps = set(best.values())
            for dep in op.deps:
                dep.marked = True
        counters = {}
        handles = {}

        def get_sem(name):
            if name not in handles:
                handles[name] = ctx.enter_context(nc.semaphore(name))
                counters[name] = 0
            return name

        for op in self.ops:
            if op.dma:
                op.sem = get_sem(op.stream)
            elif op.marked:
                name = get_sem(ENG_SEM[op.eng])
                counters[name] += 1
                op.sem = name
                op.val = counters[name]

        with nc.Block() as block:
            for engname in ("tensor", "vector", "scalar", "gpsimd", "sync"):
                deco = getattr(block, engname)

                def make(engname=engname):
                    def body(e):
                        waited = {}
                        for op in self.ops:
                            if op.eng != engname:
                                continue
                            wants = {}
                            for dep in op.deps:
                                if skip_edge(dep, op):
                                    continue
                                assert dep.sem is not None
                                v = dep.val
                                if (dep.dma and dep.stream is not None
                                        and self.streams[dep.stream]["barrier"]):
                                    v = self.streams[dep.stream]["count"]
                                if wants.get(dep.sem, 0) < v:
                                    wants[dep.sem] = v
                            for s in sorted(wants):
                                v = wants[s]
                                if waited.get(s, 0) >= v:
                                    continue
                                e.wait_ge(handles[s], v)
                                waited[s] = v
                            inst = op.fn(e)
                            if op.sem is not None:
                                inst.then_inc(handles[op.sem], 16 if op.dma else 1)
                    return body

                deco(make())


def r3(ap, inner):
    return ap.rearrange("p (a b) -> p a b", b=inner)


DEBUG = False


def build_launch1():
    nc = bass.Bass("TRN2", target_bir_lowering=False, detect_race_conditions=False)
    ctx = ExitStack()

    x_d = nc.dram_tensor("x", [T, D], F32, kind="ExternalInput").ap()
    wq_d = nc.dram_tensor("wq", [D, D], F32, kind="ExternalInput").ap()
    wk_d = nc.dram_tensor("wk", [D, D], F32, kind="ExternalInput").ap()
    wvT_d = nc.dram_tensor("wvT", [D, D], F32, kind="ExternalInput").ap()
    wfa_d = nc.dram_tensor("wfa", [2 * 128, D], F32, kind="ExternalInput").ap()
    woa_d = nc.dram_tensor("woa", [2 * 128, D], F32, kind="ExternalInput").ap()
    wmapqT_d = nc.dram_tensor("wmapqT", [DH, M], F32, kind="ExternalInput").ap()
    wmapkT_d = nc.dram_tensor("wmapkT", [DH, M], F32, kind="ExternalInput").ap()
    bq_c_d = nc.dram_tensor("bq_c", [DH, H], F32, kind="ExternalInput").ap()
    bk_c_d = nc.dram_tensor("bk_c", [DH, H], F32, kind="ExternalInput").ap()
    bv_r_d = nc.dram_tensor("bv_r", [1, D], F32, kind="ExternalInput").ap()
    lnw_c_d = nc.dram_tensor("lnw_c", [DH, NK], F32, kind="ExternalInput").ap()
    lnb_c_d = nc.dram_tensor("lnb_c", [DH, NK], F32, kind="ExternalInput").ap()
    ident_d = nc.dram_tensor("ident", [128, 128], BF16, kind="ExternalInput").ap()

    qp_d = nc.dram_tensor("qp", [128, 8 * T], BF16, kind="ExternalOutput").ap()
    xn2_d = nc.dram_tensor("xn2", [128, NK * T], BF16, kind="ExternalOutput").ap()
    kvm_d = nc.dram_tensor("kvm", [128, 1024], BF16, kind="ExternalOutput").ap()
    kvs_d = nc.dram_tensor("kvs", [128, 16], F32, kind="ExternalOutput").ap()
    amax_d = nc.dram_tensor("amax", [1, 1], F32, kind="ExternalOutput").ap()
    wfb_d = nc.dram_tensor("wfb", [128, 2 * D], BF16, kind="ExternalOutput").ap()
    wob_d = nc.dram_tensor("wob", [128, 2 * D], BF16, kind="ExternalOutput").ap()

    weq_d = nc.dram_tensor("weq", [128, NK * H * M], BF16, kind="Internal")
    wek_d = nc.dram_tensor("wek", [128, NK * H * M], BF16, kind="Internal")
    if DEBUG:
        kptm_dbg = nc.dram_tensor("kptm_dbg", [128, NT * 1024], BF16, kind="ExternalOutput").ap()
        stats_dbg = nc.dram_tensor("stats_dbg", [128, NT], F32, kind="ExternalOutput").ap()

    A = nc.alloc_sbuf_tensor
    ident = A("identt", [128, 128], BF16).ap()
    ones_col_b = A("ones_col_b", [128, 1], BF16).ap()
    ones_col_f = A("ones_col_f", [128, 1], F32).ap()
    ones_row_f = A("ones_row_f", [1, 128], F32).ap()
    ones_row_bb = A("ones_row_bb", [1, 128], BF16).ap()
    ones512_b = A("ones512_b", [1, 512], BF16).ap()
    eps1_col = A("eps1_col", [128, 1], F32).ap()
    nhalf_col = A("nhalf_col", [128, 1], F32).ap()
    wmapq = A("wmapq", [DH, M], BF16).ap()
    wmapk = A("wmapk", [DH, M], BF16).ap()
    bq_c = A("bq_cs", [DH, H], BF16).ap()
    bk_c = A("bk_cs", [DH, H], BF16).ap()
    bv_r = A("bv_rs", [1, D], BF16).ap()
    lnw_c = A("lnw_cs", [DH, NK], F32).ap()
    lnb_c = A("lnb_cs", [DH, NK], F32).ap()

    xs = [A(f"xs{j}", [128, D], BF16).ap() for j in range(3)]
    wfa = A("wfas", [128, D], BF16).ap()
    am_tmp = A("am_tmp", [128, 1], F32).ap()
    woa = A("woas", [128, D], BF16).ap()
    am_col = A("am_col", [128, 1], F32).ap()
    am_cb = A("am_cb", [128, 1], BF16).ap()
    am1 = A("am1", [1, 1], F32).ap()
    xn_tm = A("xn_tm", [128, NT * D], BF16).ap()
    whs = [A(f"whs{j}", [128, D], BF16).ap() for j in range(5)]
    stg = [A(f"stg{j}", [128, NK * 512], BF16).ap() for j in range(2)]
    effs = [A(f"effs{j}", [128, NK * 128], BF16).ap() for j in range(3)]

    st24 = [A(f"st24_{j}", [128, 24], F32).ap() for j in range(2)]
    mv = [A(f"mv{j}", [128, 2], F32).ap() for j in range(2)]
    r1c = [A(f"r1c{j}", [128, 1], F32).ap() for j in range(2)]
    tca = [A(f"tca{j}", [128, 1], F32).ap() for j in range(2)]
    tcb = [A(f"tcb{j}", [128, 1], F32).ap() for j in range(2)]
    r2cs = A("r2cs", [128, NT], F32).ap()

    xn_fm = A("xn_fm", [128, NK * T], BF16).ap()
    elu1 = A("elu1", [128, T], BF16).ap()
    qp = A("qps", [128, 2 * T], BF16).ap()
    kp_tm = A("kp_tm", [128, NT * 1024], BF16).ap()
    Ats = [A(f"Ats{j}", [128, 1024], BF16).ap() for j in range(2)]
    ksum_cf = A("ksum_cf", [M, H], F32).ap()
    ksum_rb = A("ksum_rb", [1, H * M], BF16).ap()
    kv_sb = A("kv_sbs", [128, 1040], F32).ap()
    diag = A("diagt", [128, 128], F32).ap()
    bq_row = A("bq_row", [1, H * M], BF16).ap()
    bk_row = A("bk_row", [1, H * M], BF16).ap()
    padq = [A(f"padq{j}", [128, 128], BF16).ap() for j in range(1)]

    B_r2 = A("B_r2", [128, T], F32).ap()
    r2row = A("r2row", [1, T], F32).ap()
    prep_st = [A(f"prep_st{j}", [128, NK * M], BF16).ap() for j in range(2)]

    APs = nc.alloc_psum_tensor
    ps_ab = APs("ps_ab", [128, 1024], F32).ap()
    ps_cd = APs("ps_cd", [128, 1024], F32).ap()
    ps_ef = APs("ps_ef", [128, 1024], F32).ap()
    ps_t1 = APs("ps_t1", [128, 1024], BF16).ap()
    ps_t2 = APs("ps_t2", [128, 1024], BF16).ap()

    P = Prog(nc)

    def cdma(dst, src):
        return lambda e: e.dma_start(out=dst, in_=src)

    P.dma("gpsimd", cdma(ident[:, :], ident_d), writes=["ident"], sem="csts", barrier=True)
    P.dma("gpsimd", cdma(wmapq[:, :], wmapqT_d), writes=["wmapq"], sem="csts", barrier=True)
    P.dma("gpsimd", cdma(wmapk[:, :], wmapkT_d), writes=["wmapk"], sem="csts", barrier=True)
    P.dma("gpsimd", cdma(bq_c[:, :], bq_c_d), writes=["bq_c"], sem="csts", barrier=True)
    P.dma("gpsimd", cdma(bk_c[:, :], bk_c_d), writes=["bk_c"], sem="csts", barrier=True)
    P.dma("gpsimd", cdma(bv_r[:, :], bv_r_d), writes=["bv_r"], sem="csts", barrier=True)
    P.dma("sync", cdma(lnw_c[:, :], lnw_c_d), writes=["lnw_c"], sem="cstsy", barrier=True)
    P.dma("sync", cdma(lnb_c[:, :], lnb_c_d), writes=["lnb_c"], sem="cstsy", barrier=True)

    P.comp("vector", lambda e: e.memset(ones_col_b[:, :], 1.0), writes=["ones_col_b"])
    P.comp("vector", lambda e: e.memset(ones_col_f[:, :], 1.0), writes=["ones_col_f"])
    P.comp("vector", lambda e: e.memset(ones_row_f[:, :], 1.0), writes=["ones_row_f"])
    P.comp("vector", lambda e: e.memset(ones_row_bb[:, :], 1.0), writes=["ones_row_bb"])
    P.comp("vector", lambda e: e.memset(ones512_b[:, :], 1.0), writes=["ones512_b"])
    P.comp("vector", lambda e: e.memset(eps1_col[:, :], EPS1), writes=["eps1_col"])
    P.comp("vector", lambda e: e.memset(nhalf_col[:, :], -0.5), writes=["nhalf_col"])

    def prep_blk(w_d, mapw, mres, eff_d, tag, blk, cd_only=False, waits=()):
        sslot = blk % 2
        wres = f"stg{sslot}"
        ldop = P.dma("gpsimd",
              (lambda sslot, blk, w_d: lambda e: e.dma_start(
                  out=r3(stg[sslot], D)[:, 0:2, :],
                  in_=w_d.rearrange("(b p) k -> p b k", p=128)[:, 2 * blk:2 * blk + 2, :]))(
                  sslot, blk, w_d),
              writes=[wres], sem=f"stg{sslot}", waits=waits)
        for hh in range(2):
            h = 2 * blk + hh
            psx, pre = (ps_cd, "ps_cd") if cd_only else (
                (ps_ab, "ps_ab") if hh % 2 == 0 else (ps_cd, "ps_cd"))
            for kc in range(NK):
                P.comp("tensor",
                       (lambda sslot, hh, kc, psx: lambda e: e.matmul(
                           psx[:, kc * M:(kc + 1) * M],
                           r3(stg[sslot], D)[:, hh, kc * 128:(kc + 1) * 128],
                           mapw[:, :], start=True, stop=True))(sslot, hh, kc, psx),
                       reads=[wres, mres], writes=[f"{pre}{kc // 8}"])
            pslot = hh % 2
            P.comp("scalar",
                   (lambda psx, pslot: lambda e: e.activation(
                       prep_st[pslot][:, :], psx[:, :], AF.Copy))(psx, pslot),
                   reads=[f"{pre}0", f"{pre}1"], writes=[f"prep_st{pslot}"])
            P.dma("sync",
                  (lambda pslot, h, eff_d: lambda e: e.dma_start(
                      out=r3(eff_d.ap(), H * M)[:, :, h * M:(h + 1) * M],
                      in_=r3(prep_st[pslot], M)[:, :, :]))(pslot, h, eff_d),
                  reads=[f"prep_st{pslot}"], writes=[f"{tag}effd_h{h}"],
                  sem=f"effout{pslot}")
        return ldop

    def emit_prep_q_blk(blk, waits=()):
        return prep_blk(wq_d, wmapq, "wmapq", weq_d, "q", blk, waits=waits)

    def emit_prep_k_blk(blk, waits=()):
        return prep_blk(wk_d, wmapk, "wmapk", wek_d, "k", blk, cd_only=True, waits=waits)

    def emit_bias_rows():
        for (bc, bres, mapw, mres, brow, brres) in (
                (bq_c, "bq_c", wmapq, "wmapq", bq_row, "bq_row"),
                (bk_c, "bk_c", wmapk, "wmapk", bk_row, "bk_row")):
            for h in range(H):
                P.comp("tensor",
                       (lambda bc, mapw, h: lambda e: e.matmul(
                           ps_ef[0:1, h * M:(h + 1) * M], bc[:, h:h + 1], mapw[:, :],
                           start=True, stop=True))(bc, mapw, h),
                       reads=[bres, mres], writes=[f"ps_ef{h // 8}"])
            P.comp("scalar",
                   (lambda brow: lambda e: e.activation(brow[:, :], ps_ef[0:1, 0:1024],
                                                        AF.Copy))(brow),
                   reads=["ps_ef0", "ps_ef1"], writes=[brres])


    xn_fm3 = r3(xn_fm, T)
    xn_tm3 = r3(xn_tm, D)
    weq3d = r3(weq_d.ap(), H * M)

    def emit_effs_load(cch):
        slot = cch % 3
        P.dma("sync",
              (lambda slot, cch: lambda e: e.dma_start(
                  out=r3(effs[slot], 128)[:, :, :],
                  in_=weq3d[:, :, cch * 128:(cch + 1) * 128]))(slot, cch),
              writes=[f"effs{slot}"], reads=[f"qeffd_h{2 * cch}", f"qeffd_h{2 * cch + 1}"],
              sem=f"effs{slot}")

    for i in range(NT):
        slot = i % 3
        xr = f"xs{slot}"
        P.dma("gpsimd",
              (lambda slot, i: lambda e: e.dma_start(out=xs[slot][:, :],
                                                     in_=x_d[i * 128:(i + 1) * 128, :]))(slot, i),
              writes=[xr], sem=f"xs{slot}")
        emit_prep_q_blk(i)
        if i in (2, 3, 4):
            emit_effs_load(i - 2)
        j = i % 2
        for c in range(4):
            P.comp("vector",
                   (lambda slot, c, j: lambda e: e.bn_stats(st24[j][:, c * 6:(c + 1) * 6],
                                                            xs[slot][:, c * 512:(c + 1) * 512]))(slot, c, j),
                   reads=[xr], writes=[f"st24_{j}"])
        P.comp("vector", (lambda j: lambda e: e.bn_aggr(mv[j][:, :], st24[j][:, :]))(j),
               reads=[f"st24_{j}"], writes=[f"mv{j}"])
        P.comp("scalar", (lambda j: lambda e: e.activation(tca[j][:, :], mv[j][:, 1:2], AF.Ln,
                                                           bias=eps1_col[:, :]))(j),
               reads=[f"mv{j}", "eps1_col"], writes=[f"tca{j}"])
        P.comp("scalar", (lambda j: lambda e: e.activation(r1c[j][:, :], tca[j][:, :], AF.Exp,
                                                           scale=nhalf_col[:, :]))(j),
               reads=[f"tca{j}", "nhalf_col"], writes=[f"r1c{j}"])
        P.comp("vector", (lambda j: lambda e: e.tensor_mul(tcb[j][:, :], r1c[j][:, :], r1c[j][:, :]))(j),
               reads=[f"r1c{j}"], writes=[f"tcb{j}"])
        P.comp("vector", (lambda j: lambda e: e.tensor_mul(tcb[j][:, :], tcb[j][:, :], mv[j][:, 1:2]))(j),
               reads=[f"mv{j}"], writes=[f"tcb{j}"])
        P.comp("scalar", (lambda j: lambda e: e.activation(tcb[j][:, :], tcb[j][:, :], AF.Ln,
                                                           bias=eps1_col[:, :]))(j),
               reads=["eps1_col"], writes=[f"tcb{j}"])
        P.comp("scalar",
               (lambda i, j: lambda e: e.activation(r2cs[:, i:i + 1], tcb[j][:, :], AF.Exp,
                                                    scale=nhalf_col[:, :]))(i, j),
               reads=[f"tcb{j}", "nhalf_col"], writes=["r2cs"])
        P.comp("vector",
               (lambda slot, i, j: lambda e: e.tensor_scalar(xn_tm3[:, i, :], xs[slot][:, :],
                                                             mv[j][:, 0:1], r1c[j][:, :],
                                                             ALU.subtract, ALU.mult))(slot, i, j),
               reads=[xr, f"mv{j}", f"r1c{j}"], writes=[f"xnt{i}"])
        for half in range(2):
            pst, pr = (ps_t1, "ps_t1") if half == 0 else (ps_t2, "ps_t2")
            for kk in range(8):
                kc = half * 8 + kk
                P.comp("tensor",
                       (lambda pst, kk, kc, i: lambda e: e.transpose(
                           pst[:, kk * 128:(kk + 1) * 128],
                           xn_tm3[:, i, kc * 128:(kc + 1) * 128], ident[:, :]))(pst, kk, kc, i),
                       reads=[f"xnt{i}", "ident"], writes=[pr])
            P.comp("vector",
                   (lambda pst, half, i: lambda e: e.tensor_copy(
                       xn_fm3[:, half * 8:(half + 1) * 8, i * 128:(i + 1) * 128],
                       r3(pst, 128)[:, :, :]))(pst, half, i),
                   reads=[pr], writes=[f"xnf{i}"])

    XNF_ALL = [f"xnf{i}" for i in range(NT)]
    XNT_ALL = [f"xnt{i}" for i in range(NT)]
    emit_bias_rows()

    qp3 = r3(qp, T)
    kp_tm3 = r3(kp_tm, 1024)

    def elu_into(dst_ap_fn, dst_res, psx, pre):
        P.comp("vector",
               (lambda psx, dst_ap_fn: lambda e: e.tensor_scalar(dst_ap_fn(), psx[:, :], 0.0, None,
                                                                 ALU.max))(psx, dst_ap_fn),
               reads=[f"{pre}0", f"{pre}1"], writes=[dst_res])
        P.comp("vector",
               (lambda psx: lambda e: e.tensor_scalar(elu1[:, :], psx[:, :], 0.0, None,
                                                      ALU.min))(psx),
               reads=[f"{pre}0", f"{pre}1"], writes=["elu1"])
        P.comp("scalar", lambda e: e.activation(elu1[:, :], elu1[:, :], AF.Exp), writes=["elu1"])
        P.comp("vector",
               (lambda dst_ap_fn: lambda e: e.tensor_add(dst_ap_fn(), dst_ap_fn(), elu1[:, :]))(dst_ap_fn),
               reads=["elu1"], writes=[dst_res])

    for cch in range(8):
        emit_prep_k_blk(cch)
        slot = cch % 3
        eres = f"effs{slot}"
        psx, pre = (ps_ab, "ps_ab") if cch % 2 == 0 else (ps_ef, "ps_ef")
        for kc in range(NK):
            for half in range(2):
                P.comp("tensor",
                       (lambda slot, kc, half, psx: lambda e: e.matmul(
                           psx[:, half * 512:(half + 1) * 512],
                           effs[slot][:, kc * 128:(kc + 1) * 128],
                           xn_fm3[:, kc, half * 512:(half + 1) * 512],
                           start=(kc == 0), stop=False))(slot, kc, half, psx),
                       reads=[eres] + XNF_ALL[half * 4:(half + 1) * 4], writes=[f"{pre}{half}"])
        for half in range(2):
            P.comp("tensor",
                   (lambda cch, half, psx: lambda e: e.matmul(
                       psx[:, half * 512:(half + 1) * 512],
                       bq_row[:, cch * 128:(cch + 1) * 128],
                       ones512_b[:, :],
                       start=False, stop=True))(cch, half, psx),
                   reads=["bq_row", "ones512_b"], writes=[f"{pre}{half}"])
        qslot = cch % 2
        elu_into((lambda qslot=qslot: qp3[:, qslot, :]), f"qpc{qslot}", psx, pre)
        P.dma("sync",
              (lambda cch, qslot: lambda e: e.dma_start(
                  out=qp_d[:, cch * T:(cch + 1) * T], in_=qp3[:, qslot, :]))(cch, qslot),
              reads=[f"qpc{qslot}"], sem=f"outq{qslot}")
        if cch + 3 < 8:
            emit_effs_load(cch + 3)

    wfa3in = wfa_d.rearrange("(c p) o -> p c o", p=128)
    P.dma("gpsimd", lambda e: e.dma_start(out=wfa[:, :], in_=wfa3in[:, 0, :]),
          writes=["wfas"], sem="wfal")
    P.comp("vector",
           lambda e: e.reduce_max(am_col[:, :], wfa[:, :], axis=AXX,
                                  apply_absolute_value=True),
           reads=["wfas"], writes=["am_col"])
    wfb_out0 = P.dma("sync", lambda e: e.dma_start(out=wfb_d[:, 0:D], in_=wfa[:, :]),
                     reads=["wfas"], sem="outa")
    woa3in = woa_d.rearrange("(c p) o -> p c o", p=128)
    P.dma("gpsimd", lambda e: e.dma_start(out=woa[:, :], in_=woa3in[:, 0, :]),
          writes=["woas"], sem="woal")
    wob_out0 = P.dma("sync", lambda e: e.dma_start(out=wob_d[:, 0:D], in_=woa[:, :]),
                     reads=["woas"], sem="outa")

    wek3d = r3(wek_d.ap(), H * M)
    for j in range(2):
        P.dma("sync",
              (lambda j: lambda e: e.dma_start(
                  out=r3(stg[j], 1024)[:, :, :],
                  in_=wek3d[:, j * 8:(j + 1) * 8, :]))(j),
              writes=[f"stg{j}"], reads=[f"keffd_h{h}" for h in range(H)], sem=f"stg{j}")
    stgk = [r3(stg[0], 1024), r3(stg[1], 1024)]
    for i in range(NT):
        psx, pre = (ps_ab, "ps_ab") if i % 2 == 0 else (ps_ef, "ps_ef")
        for kc in range(NK):
            j, kk = (0, kc) if kc < 8 else (1, kc - 8)
            for half in range(2):
                P.comp("tensor",
                       (lambda i, kc, half, psx, j, kk: lambda e: e.matmul(
                           psx[:, half * 512:(half + 1) * 512],
                           xn_fm3[:, kc, i * 128:(i + 1) * 128],
                           stgk[j][:, kk, half * 512:(half + 1) * 512],
                           start=(kc == 0), stop=False))(i, kc, half, psx, j, kk),
                       reads=[f"xnf{i}", f"stg{j}"], writes=[f"{pre}{half}"])
        for half in range(2):
            P.comp("tensor",
                   (lambda half, psx: lambda e: e.matmul(
                       psx[:, half * 512:(half + 1) * 512],
                       ones_row_bb[0:1, 0:128],
                       bk_row[:, half * 512:(half + 1) * 512],
                       start=False, stop=True))(half, psx),
                   reads=["bk_row", "ones_row_bb"], writes=[f"{pre}{half}"])
        elu_into((lambda i=i: kp_tm3[:, i, :]), f"kpt{i}", psx, pre)
    KPT_ALL = [f"kpt{i}" for i in range(NT)]

    fillers = []
    _lists = {"r2": [], "xn2": [], "ksum": [], "conv": []}
    _cur = ["r2"]

    def F(fn):
        _lists[_cur[0]].append(fn)

    for i in range(0, NT, 2):
        def r2_pair(i=i):
            for ii in (i, i + 1):
                P.comp("vector",
                       (lambda ii: lambda e: e.tensor_scalar(diag[:, :], ident[:, :],
                                                             r2cs[:, ii:ii + 1], None,
                                                             ALU.mult))(ii),
                       reads=["ident", "r2cs"], writes=["diagr"])
                P.comp("tensor",
                       (lambda ii: lambda e: e.matmul(ps_cd[0:1, ii * 128:(ii + 1) * 128],
                                                      ones_col_f[:, :], diag[:, :],
                                                      start=True, stop=True))(ii),
                       reads=["diagr", "ones_col_f"], writes=[f"ps_cd{ii // 4}"])
        F(r2_pair)

    def r2_finish():
        P.comp("scalar", lambda e: e.activation(r2row[:, :], ps_cd[0:1, 0:1024], AF.Copy),
               reads=["ps_cd0", "ps_cd1"], writes=["r2row"])
        for half in range(2):
            P.comp("tensor",
                   (lambda half: lambda e: e.matmul(
                       ps_cd[:, half * 512:(half + 1) * 512], ones_row_f[:, :],
                       r2row[:, half * 512:(half + 1) * 512], start=True, stop=True))(half),
                   reads=["r2row", "ones_row_f"], writes=[f"ps_cd{half}"])
            P.comp("vector",
                   (lambda half: lambda e: e.tensor_copy(
                       B_r2[:, half * 512:(half + 1) * 512],
                       ps_cd[:, half * 512:(half + 1) * 512]))(half),
                   reads=[f"ps_cd{half}"], writes=["B_r2"])
    F(r2_finish)



    _cur[0] = "xn2"
    for kc in range(NK):
        def xn2_pair(kc=kc):
            P.comp("vector",
                   (lambda kc: lambda e: e.tensor_mul(xn_fm3[:, kc, :], xn_fm3[:, kc, :],
                                                      B_r2[:, :]))(kc),
                   reads=["B_r2"] + XNF_ALL, writes=[f"x2f{kc}"])
            P.comp("vector",
                   (lambda kc: lambda e: e.tensor_scalar(xn_fm3[:, kc, :], xn_fm3[:, kc, :],
                                                         lnw_c[:, kc:kc + 1], lnb_c[:, kc:kc + 1],
                                                         ALU.mult, ALU.add))(kc),
                   reads=["lnw_c", "lnb_c"], writes=[f"x2f{kc}"])
        F(xn2_pair)
        if kc == 7 or kc == 15:
            def xn2_out(j=kc // 8):
                xn2_outs.append(P.dma("sync",
                                      (lambda j: lambda e: e.dma_start(
                                          out=xn2_d[:, j * 8 * T:(j + 1) * 8 * T],
                                          in_=xn_fm[:, j * 8 * T:(j + 1) * 8 * T]))(j),
                                      reads=[f"x2f{k}" for k in range(j * 8, (j + 1) * 8)],
                                      sem="outx"))
            F(xn2_out)

    _cur[0] = "ksum"
    for h in range(H):
        def ksum_head(h=h):
            for i in range(NT):
                P.comp("tensor",
                       (lambda i, h: lambda e: e.matmul(
                           ps_cd[0:64, h:h + 1],
                           kp_tm3[:, i, h * M:(h + 1) * M], ones_col_b[:, :],
                           start=(i == 0), stop=(i == NT - 1)))(i, h),
                       reads=[f"kpt{i}", "ones_col_b"], writes=["ps_cd0"])
        F(ksum_head)

    def ksum_evac():
        P.comp("vector", lambda e: e.tensor_copy(ksum_cf[:, :], ps_cd[0:64, 0:16]),
               reads=["ps_cd0"], writes=["ksum_cf"])
        P.comp("vector", lambda e: e.memset(kv_sb[:, 1024:1040], 0.0), writes=["kv_sb"])
        P.comp("vector", lambda e: e.tensor_copy(kv_sb[0:64, 1024:1040], ksum_cf[:, :]),
               reads=["ksum_cf"], writes=["kv_sb"])
    F(ksum_evac)

    _cur[0] = "conv"
    for h in range(0, H, 2):
        def conv_pair(h=h):
            for hh in (h, h + 1):
                P.comp("vector",
                       (lambda hh: lambda e: e.tensor_scalar(diag[0:64, 0:64], ident[0:64, 0:64],
                                                             ksum_cf[:, hh:hh + 1], None,
                                                             ALU.mult))(hh),
                       reads=["ident", "ksum_cf"], writes=["diagr"])
                P.comp("tensor",
                       (lambda hh: lambda e: e.matmul(ps_cd[0:1, hh * M:(hh + 1) * M],
                                                      ones_col_f[0:64, :], diag[0:64, 0:64],
                                                      start=True, stop=True))(hh),
                       reads=["diagr", "ones_col_f"], writes=[f"ps_cd{hh // 8}"])
        F(conv_pair)

    def ksum_row_evac():
        P.comp("scalar", lambda e: e.activation(ksum_rb[:, :], ps_cd[0:1, 0:1024], AF.Copy),
               reads=["ps_cd0", "ps_cd1"], writes=["ksum_rb"])
    F(ksum_row_evac)


    fillers.extend(_lists["r2"])
    xl = _lists["xn2"]
    kl = _lists["ksum"] + _lists["conv"]
    xi = ki = 0
    while xi < len(xl) or ki < len(kl):
        if xi < len(xl):
            fillers.append(xl[xi]); xi += 1
            if xi < len(xl) and xi in (8, 17):
                fillers.append(xl[xi]); xi += 1
        for _ in range(2):
            if ki < len(kl):
                fillers.append(kl[ki]); ki += 1

    xn2_outs = []
    xn_tmD = r3(xn_tm, D)
    def emit_wv_load(kc):
        slot = kc % 5
        P.dma("gpsimd",
              (lambda slot, kc: lambda e: e.dma_start(out=whs[slot][:, :],
                                                      in_=wvT_d[kc * 128:(kc + 1) * 128, :]))(slot, kc),
              writes=[f"whs{slot}"], sem=f"whs{slot}")

    for kc in range(5):
        emit_wv_load(kc)
    P.dma("gpsimd", lambda e: e.dma_start(out=wfa[:, :], in_=wfa3in[:, 1, :]),
          reads=["wfas"], writes=["wfas"], sem="wfal")
    P.comp("vector",
           lambda e: e.reduce_max(am_tmp[:, :], wfa[:, :], axis=AXX,
                                  apply_absolute_value=True),
           reads=["wfas"], writes=["am_tmp"])
    P.comp("vector", lambda e: e.tensor_max(am_col[:, :], am_col[:, :], am_tmp[:, :]),
           reads=["am_tmp"], writes=["am_col"])
    wfb_out1 = P.dma("sync", lambda e: e.dma_start(out=wfb_d[:, D:2 * D], in_=wfa[:, :]),
                     reads=["wfas"], sem="outa")
    P.dma("gpsimd", lambda e: e.dma_start(out=woa[:, :], in_=woa3in[:, 1, :]),
          reads=["woas"], writes=["woas"], sem="woal")
    for kc in range(NK):
        for _ in range(3):
            if fillers:
                fillers.pop(0)()
        slot = kc % 5
        wres = f"whs{slot}"
        psx, pre = (ps_ab, "ps_ab") if kc % 2 == 0 else (ps_ef, "ps_ef")
        aslot_ = kc % 2
        for half in range(2):
            for i in range(NT):
                P.comp("tensor",
                       (lambda kc, half, i, psx: lambda e: e.matmul(
                           psx[:, half * 512:(half + 1) * 512],
                           xn_tmD[:, i, kc * 128:(kc + 1) * 128],
                           kp_tm3[:, i, half * 512:(half + 1) * 512],
                           start=(i == 0), stop=(i == NT - 1)))(kc, half, i, psx),
                       reads=[f"xnt{i}", f"kpt{i}"], writes=[f"{pre}{half}"])
            P.comp("vector",
                   (lambda psx, aslot_, half: lambda e: e.tensor_copy(
                       Ats[aslot_][:, half * 512:(half + 1) * 512],
                       psx[:, half * 512:(half + 1) * 512]))(psx, aslot_, half),
                   reads=[f"{pre}{half}"], writes=[f"Ats{aslot_}h{half}"])
        aslot = kc % 2
        kvp, kvr = (ps_ef, "ps_ef") if kc % 2 == 0 else (ps_ab, "ps_ab")
        for h in range(H):
            r0 = 0 if h < 8 else 64
            P.comp("tensor",
                   (lambda h, aslot, slot, kvp, r0: lambda e: e.matmul(
                       kvp[r0:r0 + 64, (h % 8) * 128:(h % 8) * 128 + 128],
                       Ats[aslot][:, h * M:(h + 1) * M],
                       whs[slot][:, h * 128:(h + 1) * 128],
                       start=True, stop=True))(h, aslot, slot, kvp, r0),
                   reads=[f"Ats{aslot}h{h // 8}", wres],
                   writes=[f"{kvr}{(h % 8) // 4}"])
        if kc == 0:
            P.comp("vector",
                   (lambda kvp: lambda e: e.tensor_copy(kv_sb[:, 0:1024], kvp[:, :]))(kvp),
                   reads=[f"{kvr}0", f"{kvr}1"], writes=["kv_sb"])
        else:
            P.comp("vector",
                   (lambda kvp: lambda e: e.tensor_add(kv_sb[:, 0:1024], kv_sb[:, 0:1024],
                                                       kvp[:, :]))(kvp),
                   reads=[f"{kvr}0", f"{kvr}1"], writes=["kv_sb"])
        if kc + 5 < NK:
            emit_wv_load(kc + 5)
    while fillers:
        fillers.pop(0)()

    for h in range(H):
        r0 = 0 if h < 8 else 64
        P.comp("tensor",
               (lambda h, r0: lambda e: e.matmul(
                   ps_cd[r0:r0 + 64, (h % 8) * 128:(h % 8) * 128 + 128],
                   ksum_rb[:, h * M:(h + 1) * M],
                   bv_r[:, h * 128:(h + 1) * 128],
                   start=True, stop=True))(h, r0),
               reads=["ksum_rb", "bv_r"], writes=[f"ps_cd{(h % 8) // 4}"])
    P.comp("vector",
           lambda e: e.tensor_add(kv_sb[:, 0:1024], kv_sb[:, 0:1024], ps_cd[:, :]),
           reads=["ps_cd0", "ps_cd1"], writes=["kv_sb"])
    t_kv_out = P.dma("gpsimd", lambda e: e.dma_start(out=kvm_d, in_=kv_sb[:, 0:1024]),
                     reads=["kv_sb"], sem="outk")
    t_ks_out = P.dma("sync", lambda e: e.dma_start(out=kvs_d, in_=kv_sb[:, 1024:1040]),
                     reads=["kv_sb"], sem="outk")

    while fillers:
        fillers.pop(0)()

    P.comp("scalar", lambda e: e.activation(am_cb[:, :], am_col[:, :], AF.Copy),
           reads=["am_col"], writes=["am_cb"])
    P.comp("tensor", lambda e: e.matmul(ps_cd[0:1, 0:128], am_cb[:, :], ident[:, :],
                                        start=True, stop=True),
           reads=["am_cb", "ident"], writes=["ps_cd0"])
    P.comp("vector", lambda e: e.reduce_max(am1[:, :], ps_cd[0:1, 0:128], axis=AXX),
           reads=["ps_cd0"], writes=["am1"])
    amax_out = P.dma("sync", lambda e: e.dma_start(out=amax_d, in_=am1[:, :]),
                     reads=["am1"], sem="outa")
    wob_out = P.dma("sync", lambda e: e.dma_start(out=wob_d[:, D:2 * D], in_=woa[:, :]),
                    reads=["woas"], sem="outa")

    dbg_outs = []
    if DEBUG:
        dbg_outs.append(P.dma("sync", lambda e: e.dma_start(out=kptm_dbg, in_=kp_tm[:, :]),
                              reads=KPT_ALL, sem="outdbg"))
        dbg_outs.append(P.dma("sync", lambda e: e.dma_start(out=stats_dbg, in_=r2cs[:, :]),
                              reads=["r2cs"], sem="outdbg"))
    qp_outs = [op for op in P.ops if op.dma and op.stream and op.stream.startswith('outq')]
    P.comp("sync", lambda e: e.nop(),
           waits=[t_kv_out, t_ks_out, amax_out, wfb_out0, wfb_out1, wob_out0, wob_out] + qp_outs + xn2_outs + dbg_outs)

    P.emit(ctx)
    return nc



def build_launch2():
    nc = bass.Bass("TRN2", target_bir_lowering=False, detect_race_conditions=False)
    ctx = ExitStack()

    kvallm_d = nc.dram_tensor("kvallm", [NCORES * 128, 1024], BF16, kind="ExternalInput").ap()
    kvalls_d = nc.dram_tensor("kvalls", [NCORES * 128, 16], F32, kind="ExternalInput").ap()
    qp_d = nc.dram_tensor("qp", [128, 8 * T], BF16, kind="ExternalInput").ap()
    xn2_d = nc.dram_tensor("xn2", [128, NK * T], BF16, kind="ExternalInput").ap()
    x_d = nc.dram_tensor("x", [T, D], F32, kind="ExternalInput").ap()
    wfb_d = nc.dram_tensor("wfb", [128, NK * D], BF16, kind="ExternalInput").ap()
    wob_d = nc.dram_tensor("wob", [128, NK * D], BF16, kind="ExternalInput").ap()
    bo_r_d = nc.dram_tensor("bo_r", [1, D], F32, kind="ExternalInput").ap()
    gamma_r_d = nc.dram_tensor("gamma_r", [1, D], F32, kind="ExternalInput").ap()
    beta_r_d = nc.dram_tensor("beta_r", [1, D], F32, kind="ExternalInput").ap()
    amax8_d = nc.dram_tensor("amax8", [1, NCORES], F32, kind="ExternalInput").ap()
    out_d = nc.dram_tensor("out", [T, D], F32, kind="ExternalOutput").ap()

    A = nc.alloc_sbuf_tensor
    ones_row_f = A("ones_row_f", [1, 128], F32).ap()
    ones_row_b2 = A("ones_row_b2", [1, 128], BF16).ap()
    nEPS = A("nEPS", [1, 1], F32).ap()
    nm1 = A("nm1", [1, 1], F32).ap()
    kv_acc = A("kv_acc", [128, 1040], F32).ap()
    kvbig = A("kvbig", [128, 4 * 1024], BF16).ap()
    ksbig = A("ksbig", [128, NCORES * 16], F32).ap()
    kvb = A("kvb", [128, 2048], BF16).ap()
    ksum_cb = A("ksum_cb", [128, H], BF16).ap()
    qp = A("qps", [128, 8 * T], BF16).ap()
    xn2 = A("xn2s", [128, NK * T], BF16).ap()
    att = A("atts", [128, NK * T], BF16).ap()
    gob = [A(f"gob{j}", [128, NT * 512], BF16).ap() for j in range(2)]
    wsg = [A(f"wsg{j}", [128, NK * 512], BF16).ap() for j in range(2)]
    B_gs = A("B_gs", [128, D], BF16).ap()
    B_beta = A("B_beta", [128, D], BF16).ap()
    B_bo = A("B_bo", [128, D], BF16).ap()
    fsc = [A(f"fsc{j}", [128, 512], F32).ap() for j in range(2)]
    gamma_r = A("gamma_rs", [1, D], BF16).ap()
    beta_r = A("beta_rs", [1, D], BF16).ap()
    bo_r = A("bo_rs", [1, D], BF16).ap()
    am8 = A("am8", [1, NCORES], F32).ap()
    am_sc = A("am_sc", [1, 1], F32).ap()
    scale_col = A("scale_col", [128, 1], F32).ap()
    lnr = [A(f"lnr{j}", [1, T], F32).ap() for j in range(2)]
    lnrb = [A(f"lnrb{j}", [1, T], BF16).ap() for j in range(2)]
    rbs = [A(f"rbs{j}", [128, T], F32).ap() for j in range(1)]
    xsb = [A(f"xsb{j}", [128, 512], F32).ap() for j in range(2)]
    osc = [A(f"osc{j}", [128, 512], F32).ap() for j in range(2)]

    APs = nc.alloc_psum_tensor
    ps_a = APs("ps_a", [128, 1024], F32).ap()
    ps_b = APs("ps_b", [128, 1024], F32).ap()
    ps_s = APs("ps_s", [128, 1024], F32).ap()
    ps_n = APs("ps_n", [1, 1024], F32).ap()

    P = Prog(nc)

    wob3 = r3(wob_d, D)
    wfb3 = r3(wfb_d, D)
    qp3 = r3(qp, T)
    xn23 = r3(xn2, T)
    att3 = r3(att, T)

    P.comp("vector", lambda e: e.memset(ones_row_f[:, :], 1.0), writes=["ones_row_f"])
    P.comp("vector", lambda e: e.memset(ones_row_b2[:, :], 1.0), writes=["ones_row_b2"])
    P.comp("vector", lambda e: e.memset(nEPS[:, :], EPS2), writes=["nEPS"])
    P.comp("vector", lambda e: e.memset(nm1[:, :], -1.0), writes=["nm1"])

    P.dma("gpsimd", lambda e: e.dma_start(out=bo_r[:, :], in_=bo_r_d), writes=["bo_r"], sem="csts", barrier=True)
    P.dma("gpsimd", lambda e: e.dma_start(out=gamma_r[:, :], in_=gamma_r_d), writes=["gamma_r"], sem="csts", barrier=True)
    P.dma("gpsimd", lambda e: e.dma_start(out=beta_r[:, :], in_=beta_r_d), writes=["beta_r"], sem="csts", barrier=True)
    t_wf0 = P.dma("gpsimd", lambda e: e.dma_start(out=r3(wsg[0], 512)[:, :, :],
                                                  in_=wfb3[:, :, 0:512]),
                  writes=["wsg0"], sem="wsg0")
    t_xn2 = P.dma("scalar", lambda e: e.dma_start(out=xn2[:, :], in_=xn2_d), writes=["xn2l"], sem="xnl")
    P.dma("scalar", lambda e: e.dma_start(out=qp[:, :], in_=qp_d), writes=["qp"], sem="qpl",
          waits=[t_wf0])

    P.dma("sync", lambda e: e.dma_start(out=am8[:, :], in_=amax8_d), writes=["am8"], sem="aml")
    P.comp("vector", lambda e: e.reduce_max(am_sc[:, :], am8[:, :], axis=AXX),
           reads=["am8"], writes=["am_sc"])
    P.comp("tensor", lambda e: e.matmul(ps_b[0:128, 0:1], ones_row_f[:, :], am_sc[:, :],
                                        start=True, stop=True),
           reads=["am_sc", "ones_row_f"], writes=["ps_b0"])
    P.comp("vector", lambda e: e.tensor_copy(scale_col[:, :], ps_b[0:128, 0:1]),
           reads=["ps_b0"], writes=["scale_col"])

    for q in range(4):
        bank = q % 2
        P.comp("tensor",
               (lambda q, bank: lambda e: e.matmul(
                   ps_b[:, bank * 512:(bank + 1) * 512], ones_row_b2[:, :],
                   bo_r[:, q * 512:(q + 1) * 512], start=True, stop=True))(q, bank),
               reads=["bo_r", "ones_row_b2"], writes=[f"ps_b{bank}"])
        P.comp("vector",
               (lambda q, bank: lambda e: e.tensor_copy(
                   B_bo[:, q * 512:(q + 1) * 512],
                   ps_b[:, bank * 512:(bank + 1) * 512]))(q, bank),
               reads=[f"ps_b{bank}"], writes=["B_bo"])
    for q in range(4):
        bank = q % 2
        P.comp("tensor",
               (lambda q, bank: lambda e: e.matmul(
                   ps_b[:, bank * 512:(bank + 1) * 512], ones_row_b2[:, :],
                   gamma_r[:, q * 512:(q + 1) * 512], start=True, stop=True))(q, bank),
               reads=["gamma_r", "ones_row_b2"], writes=[f"ps_b{bank}"])
        P.comp("vector",
               (lambda q, bank: lambda e: e.tensor_scalar(
                   B_gs[:, q * 512:(q + 1) * 512],
                   ps_b[:, bank * 512:(bank + 1) * 512],
                   scale_col[:, :], None, ALU.mult))(q, bank),
               reads=[f"ps_b{bank}", "scale_col"], writes=["B_gs"])
    for q in range(4):
        bank = q % 2
        P.comp("tensor",
               (lambda q, bank: lambda e: e.matmul(
                   ps_b[:, bank * 512:(bank + 1) * 512], ones_row_b2[:, :],
                   beta_r[:, q * 512:(q + 1) * 512], start=True, stop=True))(q, bank),
               reads=["beta_r", "ones_row_b2"], writes=[f"ps_b{bank}"])
        P.comp("vector",
               (lambda q, bank: lambda e: e.tensor_copy(
                   B_beta[:, q * 512:(q + 1) * 512],
                   ps_b[:, bank * 512:(bank + 1) * 512]))(q, bank),
               reads=[f"ps_b{bank}"], writes=["B_beta"])

    kvm3 = kvallm_d.rearrange("(c p) o -> p c o", p=128)
    kvb3 = r3(kvbig, 1024)
    t_kvm1 = P.dma("sync", lambda e: e.dma_start(out=kvb3[:, :, :], in_=kvm3[:, 0:4, :]),
                   writes=["kvbig"], sem="kva", waits=[t_wf0, t_xn2])
    P.dma("sync", lambda e: e.dma_start(out=r3(ksbig, 16)[:, :, :],
                                        in_=kvalls_d.rearrange("(c p) o -> p c o", p=128)),
          writes=["ksbig"], sem="ksl")
    P.comp("gpsimd", lambda e: e.tensor_copy(kv_acc[:, 0:1024], kvb3[:, 0, :]),
           reads=["kvbig"], writes=["kv_acc"])
    for c in range(1, 4):
        P.comp("gpsimd",
               (lambda c: lambda e: e.tensor_add(kv_acc[:, 0:1024], kv_acc[:, 0:1024],
                                                 kvb3[:, c, :]))(c),
               reads=["kvbig"], writes=["kv_acc"])
    P.dma("sync", lambda e: e.dma_start(out=kvb3[:, :, :], in_=kvm3[:, 4:8, :]),
          writes=["kvbig"], sem="kva")
    for c in range(4):
        P.comp("gpsimd",
               (lambda c: lambda e: e.tensor_add(kv_acc[:, 0:1024], kv_acc[:, 0:1024],
                                                 kvb3[:, c, :]))(c),
               reads=["kvbig"], writes=["kv_acc"])
    P.comp("gpsimd", lambda e: e.tensor_copy(kv_acc[:, 1024:1040], r3(ksbig, 16)[:, 0, :]),
           reads=["ksbig"], writes=["kv_acc"])
    for c in range(1, NCORES):
        P.comp("gpsimd",
               (lambda c: lambda e: e.tensor_add(kv_acc[:, 1024:1040], kv_acc[:, 1024:1040],
                                                 r3(ksbig, 16)[:, c, :]))(c),
               reads=["ksbig"], writes=["kv_acc"])
    P.comp("gpsimd", lambda e: e.tensor_copy(ksum_cb[0:64, :], kv_acc[0:64, 1024:1040]),
           reads=["kv_acc"], writes=["ksum_cb"])
    P.dma("sync", lambda e: e.dma_start(out=ksum_cb[64:128, :], in_=ksum_cb[0:64, :]),
          reads=["ksum_cb"], writes=["ksum_cb"], sem="ksd")
    ka3 = r3(kv_acc[:, 0:1024], 128)
    kv3 = r3(kvb, 128)
    P.dma("gpsimd", lambda e: e.dma_start(out=kv3[0:64, 0:8:2, :], in_=ka3[0:64, 0:8:2, :]),
          reads=["kv_acc"], writes=["kvb"], sem="dup")
    P.dma("gpsimd", lambda e: e.dma_start(out=kv3[64:128, 1:8:2, :], in_=ka3[0:64, 1:8:2, :]),
          reads=["kv_acc"], writes=["kvb"], sem="dup")
    P.dma("gpsimd", lambda e: e.dma_start(out=kv3[0:64, 8:16:2, :], in_=ka3[64:128, 0:8:2, :]),
          reads=["kv_acc"], writes=["kvb"], sem="dup")
    P.dma("gpsimd", lambda e: e.dma_start(out=kv3[64:128, 9:16:2, :], in_=ka3[64:128, 1:8:2, :]),
          reads=["kv_acc"], writes=["kvb"], sem="dup")
    P.dma("gpsimd", lambda e: e.dma_start(out=r3(wsg[1], 512)[:, :, :],
                                          in_=wob3[:, :, 0:512]),
          writes=["wsg1"], sem="wsg1")

    def emit_ffn_block(ob):
        if ob > 0:
            P.dma("gpsimd",
                  (lambda ob: lambda e: e.dma_start(
                      out=r3(wsg[0], 512)[:, :, :],
                      in_=wfb3[:, :, ob * 512:(ob + 1) * 512]))(ob),
                  writes=["wsg0"], sem="wsg0")
        P.comp("scalar", lambda e: e.activation(wsg[0][:, :], wsg[0][:, :], AF.Sign),
               writes=["wsg0"])
        gslot = ob % 2
        for i in range(NT):
            bank = i % 2
            pv = f"ps_s{bank}"
            for kc in range(NK):
                P.comp("tensor",
                       (lambda i, kc, bank: lambda e: e.matmul(
                           ps_s[:, bank * 512:(bank + 1) * 512],
                           xn23[:, kc, i * 128:(i + 1) * 128],
                           r3(wsg[0], 512)[:, kc, :],
                           start=(kc == 0), stop=(kc == NK - 1)))(i, kc, bank),
                       reads=["xn2l", "wsg0"], writes=[pv])
            P.comp("vector",
                   (lambda i, ob, bank: lambda e: e.tensor_mul(
                       fsc[bank][:, :], ps_s[:, bank * 512:(bank + 1) * 512],
                       B_gs[:, ob * 512:(ob + 1) * 512]))(i, ob, bank),
                   reads=[pv, "B_gs"], writes=[f"fsc{bank}"])
            P.comp("vector",
                   (lambda ob, bank: lambda e: e.tensor_add(
                       fsc[bank][:, :], fsc[bank][:, :],
                       B_beta[:, ob * 512:(ob + 1) * 512]))(ob, bank),
                   reads=["B_beta"], writes=[f"fsc{bank}"])
            P.comp("scalar",
                   (lambda i, bank, gslot: lambda e: e.activation(
                       gob[gslot][:, i * 512:(i + 1) * 512], fsc[bank][:, :],
                       AF.Silu))(i, bank, gslot),
                   reads=[f"fsc{bank}"], writes=[f"gob{gslot}"])

    def emit_attention():
        def emit_norm(h):
            p0 = (h % 2) * 64
            ch = h // 2
            hs = h % 2
            for half in range(2):
                P.comp("tensor",
                       (lambda h, half, p0, ch: lambda e: e.matmul(
                           ps_n[0:1, half * 512:(half + 1) * 512],
                           ksum_cb[p0:p0 + 64, h:h + 1],
                           qp3[p0:p0 + 64, ch, half * 512:(half + 1) * 512],
                           start=True, stop=True))(h, half, p0, ch),
                       reads=["ksum_cb", "qp"], writes=[f"ps_n{half}"])
            P.comp("scalar",
                   (lambda hs: lambda e: e.activation(
                       lnr[hs][:, :], ps_n[0:1, :], AF.Ln,
                       bias=nEPS[:, :]))(hs),
                   reads=["ps_n0", "ps_n1", "nEPS"], writes=[f"lnr{hs}"])
            P.comp("scalar",
                   (lambda hs: lambda e: e.activation(lnrb[hs][:, :], lnr[hs][:, :], AF.Exp,
                                                      scale=nm1[:, :]))(hs),
                   reads=["nm1", f"lnr{hs}"], writes=[f"lnrb{hs}"])

        def emit_apply(h):
            p0 = (h % 2) * 64
            ch = h // 2
            hs = h % 2
            for half in range(2):
                P.comp("tensor",
                       (lambda half, hs: lambda e: e.matmul(
                           ps_b[:, half * 512:(half + 1) * 512], ones_row_b2[:, :],
                           lnrb[hs][:, half * 512:(half + 1) * 512],
                           start=True, stop=True))(half, hs),
                       reads=[f"lnrb{hs}", "ones_row_b2"], writes=[f"ps_b{half}"])
            P.comp("vector",
                   lambda e: e.tensor_copy(rbs[0][:, :], ps_b[:, :]),
                   reads=["ps_b0", "ps_b1"], writes=["rbs0"])
            for half in range(2):
                P.comp("tensor",
                       (lambda h, half, p0, ch: lambda e: e.matmul(
                           ps_a[:, half * 512:(half + 1) * 512],
                           kvb[p0:p0 + 64, h * 128:(h + 1) * 128],
                           qp3[p0:p0 + 64, ch, half * 512:(half + 1) * 512],
                           start=True, stop=True))(h, half, p0, ch),
                       reads=["kvb", "qp"], writes=[f"ps_a{half}"])
            P.comp("vector",
                   (lambda h: lambda e: e.tensor_mul(
                       att3[:, h, :], ps_a[:, :], rbs[0][:, :]))(h),
                   reads=["ps_a0", "ps_a1", "rbs0"], writes=["att"])

        for h in range(H + 1):
            if h < H:
                emit_norm(h)
            if h >= 1:
                emit_apply(h - 1)

    out_dmas = []

    def emit_oproj_block(ob):
        if ob > 0:
            P.dma("gpsimd",
                  (lambda ob: lambda e: e.dma_start(
                      out=r3(wsg[1], 512)[:, :, :],
                      in_=wob3[:, :, ob * 512:(ob + 1) * 512]))(ob),
                  writes=["wsg1"], sem="wsg1")
        gslot = ob % 2
        for i in range(NT):
            bank = i % 2
            pv = f"ps_s{bank}"
            xslot = i % 2
            P.dma("sync",
                  (lambda i, ob, xslot: lambda e: e.dma_start(
                      out=xsb[xslot][:, :],
                      in_=x_d[i * 128:(i + 1) * 128, ob * 512:(ob + 1) * 512]))(i, ob, xslot),
                  writes=[f"xsb{xslot}"], sem=f"xsb{xslot}")
            for kc in range(NK):
                P.comp("tensor",
                       (lambda i, ob, kc, bank: lambda e: e.matmul(
                           ps_s[:, bank * 512:(bank + 1) * 512],
                           att3[:, kc, i * 128:(i + 1) * 128],
                           r3(wsg[1], 512)[:, kc, :],
                           start=(kc == 0), stop=(kc == NK - 1)))(i, ob, kc, bank),
                       reads=["att", "wsg1"], writes=[pv])
            P.comp("vector",
                   (lambda ob, bank: lambda e: e.tensor_add(
                       osc[bank][:, :], ps_s[:, bank * 512:(bank + 1) * 512],
                       B_bo[:, ob * 512:(ob + 1) * 512]))(ob, bank),
                   reads=[pv, "B_bo"], writes=[f"osc{bank}"])
            P.comp("vector",
                   (lambda i, bank, gslot: lambda e: e.tensor_mul(
                       osc[bank][:, :], osc[bank][:, :],
                       gob[gslot][:, i * 512:(i + 1) * 512]))(i, bank, gslot),
                   reads=[f"gob{gslot}"], writes=[f"osc{bank}"])
            P.comp("vector",
                   (lambda bank, xslot: lambda e: e.tensor_add(
                       osc[bank][:, :], osc[bank][:, :],
                       xsb[xslot][:, :]))(bank, xslot),
                   reads=[f"xsb{xslot}"], writes=[f"osc{bank}"])
            out_dmas.append(P.dma("sync",
                                  (lambda i, ob, bank: lambda e: e.dma_start(
                                      out=out_d[i * 128:(i + 1) * 128, ob * 512:(ob + 1) * 512],
                                      in_=osc[bank][:, :]))(i, ob, bank),
                                  reads=[f"osc{bank}"], sem=f"outo{bank}"))

    emit_ffn_block(0)
    emit_attention()
    emit_oproj_block(0)
    for ob in range(1, 4):
        emit_ffn_block(ob)
        emit_oproj_block(ob)

    P.comp("sync", lambda e: e.nop(), waits=out_dmas)

    P.emit(ctx)
    return nc



_NC1 = None
_NC2 = None


def kernel(**inputs):
    global _NC1, _NC2
    x_full = np.asarray(inputs["L0_attn_hidden_states"])
    wq = np.asarray(inputs["L0_attn_wq"])
    wk = np.asarray(inputs["L0_attn_wk"])
    wv = np.asarray(inputs["L0_attn_wv"])
    wo = np.asarray(inputs["L0_attn_wo"])
    bq = np.asarray(inputs["L0_attn_bq"])
    bk = np.asarray(inputs["L0_attn_bk"])
    bv = np.asarray(inputs["L0_attn_bv"])
    bo = np.asarray(inputs["L0_attn_bo"])
    wmq = np.asarray(inputs["L0_attn_wmap_q"])
    wmk = np.asarray(inputs["L0_attn_wmap_k"])
    wf = np.asarray(inputs["L0_ffn_weight"])
    gamma = np.asarray(inputs["L0_ffn_gamma"])
    beta = np.asarray(inputs["L0_ffn_beta"])
    lnw = np.asarray(inputs["L0_ffn_ln_weight"])
    lnb = np.asarray(inputs["L0_ffn_ln_bias"])

    C = np.ascontiguousarray
    wvT = C(wv.T)
    woT = C(wo.T)
    wfT = C(wf.T)
    wmapqT = C(wmq.T)
    wmapkT = C(wmk.T)
    bq_c = C(bq.reshape(H, DH).T)
    bk_c = C(bk.reshape(H, DH).T)
    lnw_c = C(lnw.reshape(NK, 128).T)
    lnb_c = C(lnb.reshape(NK, 128).T)
    bv_r = C(bv.reshape(1, D))
    bo_r = C(bo.reshape(1, D))
    gamma_r = C(gamma.reshape(1, D))
    beta_r = C(beta.reshape(1, D))
    ident = np.eye(128, dtype=BF)

    xt = x_full.reshape(8192, D)

    if _NC1 is None:
        _NC1 = build_launch1()
    in_maps1 = []
    for c in range(NCORES):
        in_maps1.append({
            "x": C(xt[c * T:(c + 1) * T, :]),
            "wq": wq, "wk": wk, "wvT": wvT,
            "wfa": C(wfT[c * 256:(c + 1) * 256, :]),
            "woa": C(woT[c * 256:(c + 1) * 256, :]),
            "wmapqT": wmapqT, "wmapkT": wmapkT,
            "bq_c": bq_c, "bk_c": bk_c, "bv_r": bv_r,
            "lnw_c": lnw_c, "lnb_c": lnb_c,
            "ident": ident,
        })
    res1 = run_bass_kernel_spmd(_NC1, in_maps1, core_ids=list(range(NCORES)))

    kvallm = np.concatenate([res1.results[c]["kvm"] for c in range(NCORES)], axis=0)
    kvalls = np.concatenate([res1.results[c]["kvs"] for c in range(NCORES)], axis=0)
    amax8 = np.concatenate([res1.results[c]["amax"] for c in range(NCORES)], axis=1)
    wfb = np.concatenate([res1.results[c]["wfb"] for c in range(NCORES)], axis=1)
    wob = np.concatenate([res1.results[c]["wob"] for c in range(NCORES)], axis=1)

    if _NC2 is None:
        _NC2 = build_launch2()
    in_maps2 = []
    for c in range(NCORES):
        in_maps2.append({
            "kvallm": kvallm,
            "kvalls": kvalls,
            "qp": res1.results[c]["qp"],
            "xn2": res1.results[c]["xn2"],
            "amax8": amax8,
            "x": C(xt[c * T:(c + 1) * T, :]),
            "wfb": wfb, "wob": wob, "bo_r": bo_r,
            "gamma_r": gamma_r, "beta_r": beta_r,
        })
    res2 = run_bass_kernel_spmd(_NC2, in_maps2, core_ids=list(range(NCORES)))

    out = np.concatenate([res2.results[c]["out"] for c in range(NCORES)], axis=0)
    return out.reshape(1, 8192, D)
